# version 14
# baseline (speedup 1.0000x reference)
import numpy as np
import concourse.bass as bass
import concourse.mybir as mybir
from concourse.bass_utils import run_bass_kernel_spmd
from concourse.aot_env import allow_ici_all_to_all

F32 = mybir.dt.float32
BF16 = mybir.dt.bfloat16
AF = mybir.ActivationFunctionType
ALU = mybir.AluOpType
AXX = mybir.AxisListType.X

B, S, D = 4, 4096, 2048
NCORES = 8
TC = S // 2
NT = TC // 128
NDM = D // 128
INTER = 5632
NIC = INTER // 128
QUARTERS = (11, 11, 11, 11)
NH = 16
EPS = 1e-5
ISQ = float(1.0 / np.sqrt(128.0))
SQ128 = float(np.sqrt(128.0))
MAGIC = 12582912.0
NEG = -30000.0

ENGINES = ("sync", "scalar", "vector", "tensor", "gpsimd")


class KB:
    def __init__(self, nc):
        self.nc = nc
        self.q = {e: [] for e in ENGINES}
        self.sem = {}
        self.cnt = {}
        self.nbar = 0

    def new_sem(self, name):
        self.sem[name] = self.nc.alloc_semaphore(name)
        self.cnt[name] = 0

    def val(self, name):
        return self.cnt[name]

    def op(self, eng, fn, waits=(), incs=()):
        waits = tuple((s, v) for s, v in waits if v > 0)
        for s, n in incs:
            self.cnt[s] += n
        incs = tuple(incs)
        sems = self.sem

        def run(e):
            for s, v in waits:
                e.wait_ge(sems[s], v)
            inst = fn(e)
            for s, n in incs:
                inst = inst.then_inc(sems[s], n)
        self.q[eng].append(run)

    def wait(self, eng, waits):
        waits = tuple((s, v) for s, v in waits if v > 0)
        if not waits:
            return
        sems = self.sem

        def run(e):
            for s, v in waits:
                e.wait_ge(sems[s], v)
        self.q[eng].append(run)

    def barrier(self, engines=ENGINES):
        self.nbar += 1
        name = f"bar{self.nbar}"
        self.new_sem(name)
        n = len(engines)
        sems = self.sem
        for eng in engines:
            def run(e, name=name, n=n):
                e.nop().then_inc(sems[name], 1)
                e.wait_ge(sems[name], n)
            self.q[eng].append(run)
        self.cnt[name] = n


class Arena:
    def __init__(self, nc, base, limit):
        self.nc, self.off, self.limit, self.n = nc, base, limit, 0

    def alloc(self, name, shape, dtype):
        sz = int(np.prod(shape[1:])) * mybir.dt.size(dtype)
        off = (self.off + 31) & ~31
        assert off + sz <= self.limit, \
            f"SBUF overflow at {name}: {off}+{sz}>{self.limit}"
        self.off = off + sz
        self.n += 1
        return self.nc.alloc_sbuf_tensor_at(name, shape, dtype, offset=off).ap()

    def mark(self):
        return self.off

    def reset(self, m):
        self.off = m


def build_program(upto=99, debug=None):
    nc = bass.Bass("TRN2", target_bir_lowering=False, debug=False,
                   detect_race_conditions=False)

    x_in = nc.dram_tensor("x", [TC, D], F32, kind="ExternalInput").ap()
    out = nc.dram_tensor("out", [TC, D], F32, kind="ExternalOutput").ap()
    WHBM = {}
    for nm, shp in (("wq", [D, D]), ("wk", [D, D]), ("wv", [D, D]),
                    ("wo", [D, D]), ("gate_w", [INTER, D]),
                    ("up_w", [INTER, D]), ("down_w", [D, INTER])):
        WHBM[nm] = nc.dram_tensor(nm, shp, F32, kind="ExternalInput").ap()
    VHBM = {}
    for nm, n in (("bq", D), ("bk", D), ("bv", D), ("bo", D), ("ln_g", D),
                  ("ln_b", D), ("gate_s", INTER), ("up_s", INTER),
                  ("down_s", D)):
        VHBM[nm] = nc.dram_tensor(nm, [n], F32, kind="ExternalInput").ap()

    x1_hbm = nc.dram_tensor("x1_hbm", [TC, D], F32).ap()
    x2_hbm = nc.dram_tensor("x2_hbm", [TC, D], F32).ap()
    cache = {nm: nc.dram_tensor(nm + "_c", [NIC, 128, D], BF16).ap()
             for nm in ("gate_w", "up_w", "down_w")}
    qkv_send = nc.dram_tensor("qkv_send", [3, 8, 2, 128, TC], BF16).ap()
    qkv_recv = nc.dram_tensor("qkv_recv", [3, 8, 2, 128, TC], BF16).ap()
    o_send = nc.dram_tensor("o_send", [8, 2, 128, TC], BF16).ap()
    o_recv = nc.dram_tensor("o_recv", [8, 2, 128, TC], BF16).ap()
    nm_bounce = nc.dram_tensor("nm_bounce", [2, 32, 128], BF16).ap()
    dbg = {}
    if debug:
        for nm, (shp, dt_) in debug.items():
            dbg[nm] = nc.dram_tensor("dbg_" + nm, shp, dt_,
                                     kind="ExternalOutput").ap()

    k = KB(nc)
    TE, VE, SE, GE = "s_te", "s_ve", "s_se", "s_ge"
    DSY, DGP, DSC, CC = "d_sy", "d_gp", "d_sc", "cc"
    for s in (TE, VE, SE, GE, DSY, DGP, DSC, CC):
        k.new_sem(s)

    probe = nc.alloc_sbuf_tensor("bprobe", [128, 32], F32)
    base = nc.lookup_mloc(probe).addr + 128
    LIMIT = base + nc.sbuf_bytes_remaining - 64
    A = Arena(nc, base, LIMIT)

    identb = A.alloc("identb", [128, 128], BF16)
    identf = A.alloc("identf", [128, 128], F32)
    rotT = A.alloc("rotT", [128, 128], BF16)
    maskA = A.alloc("maskA", [128, 128], F32)
    maskB = A.alloc("maskB", [128, 128], F32)
    ones1 = A.alloc("ones1", [1, 128], BF16)
    g_t = A.alloc("g_t", [128, NDM], F32)
    b_t = A.alloc("b_t", [128, NDM], F32)
    bias_t = {nm: A.alloc(nm + "_t", [128, NDM], F32)
              for nm in ("bq", "bk", "bv")}
    ds_t = A.alloc("ds_t", [128, NDM], F32)
    gs_t = A.alloc("gs_t", [128, NIC], F32)
    us_t = A.alloc("us_t", [128, NIC], F32)
    bo_row = A.alloc("bo_row", [1, D], BF16)
    st_a = A.alloc("st_a", [128, 16], F32)
    wnatb = [A.alloc(f"wnatb{i}", [128, D], BF16) for i in range(4)]
    zfm = A.alloc("zfm", [128, NDM * TC], BF16)

    ph = A.mark()
    xt = [A.alloc(f"xt{i}", [128, D], F32) for i in range(2)]
    rt = [A.alloc(f"rt{i}", [128, D], F32) for i in range(2)]
    ztm = [A.alloc(f"ztm{i}", [128, D], BF16) for i in range(2)]
    scr_f = A.alloc("scr_f", [128, 2048], F32)
    scr_b = A.alloc("scr_b", [128, 2048], BF16)
    A.reset(ph)
    _xt = [A.alloc(f"xtf{i}", [128, D], F32) for i in range(2)]
    _rt = [A.alloc(f"rtf{i}", [128, D], F32) for i in range(2)]
    ffn_pre = A.mark()
    wT = [A.alloc(f"wT{i}", [128, D], BF16) for i in range(4)]
    silu_b = A.alloc("silu_b", [128, 2048], BF16)
    h_ar = A.alloc("h_ar", [128, 11 * TC], BF16)
    ffn_gu_end = A.mark()
    dsl = []
    A.reset(ffn_pre)
    for j in range(5):
        dsl.append(A.alloc(f"dsl{j}", [128, D], BF16))
    A.reset(ffn_gu_end)
    for j in range(5, 11):
        dsl.append(A.alloc(f"dsl{j}", [128, D], BF16))
    ffn_end = A.mark()
    A.reset(ph)
    qwT = [A.alloc(f"qwT{i}", [128, D], BF16) for i in range(4)]
    qst = [A.alloc(f"qst{i}", [128, 1024], BF16) for i in range(2)]
    A.reset(ph)
    at_q = [A.alloc(f"at_q{i}", [128, S], BF16) for i in range(2)]
    at_k = [A.alloc(f"at_k{i}", [128, S], BF16) for i in range(2)]
    at_v = A.alloc("at_v", [128, S], BF16)
    at_vt = [A.alloc(f"at_vt{i}", [128, S], BF16) for i in range(2)]
    cos_t = A.alloc("cos_t", [128, S], BF16)
    sin_t = A.alloc("sin_t", [128, S], BF16)
    aexp_b = A.alloc("aexp_b", [128, 1024], BF16)
    at_p = [A.alloc(f"at_p{i}", [128, 512], BF16) for i in range(2)]
    at_rt = A.alloc("at_rt", [128, 512], BF16)
    at_o = [A.alloc(f"at_o{i}", [128, 512], BF16) for i in range(2)]
    at_m = A.alloc("at_m", [128, 32], F32)
    at_s = A.alloc("at_s", [128, 32], F32)
    at_t = A.alloc("at_t", [128, 8], F32)
    at_mt = A.alloc("at_mt", [128, 32], F32)
    at_hi = A.alloc("at_hi", [128, 32], BF16)
    at_lo = A.alloc("at_lo", [128, 32], BF16)
    at_nmt = A.alloc("at_nmt", [32, 256], BF16)
    at_nmh = A.alloc("at_nmh", [1, 4096], BF16)
    at_nml = A.alloc("at_nml", [1, 4096], BF16)
    att_end = A.mark()
    A.reset(ph)
    _xo = [A.alloc(f"xto{i}", [128, D], F32) for i in range(2)]
    _ro = [A.alloc(f"rto{i}", [128, D], F32) for i in range(2)]
    ofm = A.alloc("ofm", [128, 8 * TC], BF16)
    wsl = [A.alloc(f"wsl{i}", [128, D], BF16) for i in range(8)]
    oscr_f = A.alloc("oscr_f", [128, 2048], F32)
    opr_end = A.mark()
    A.reset(ph)
    assert max(ffn_end, att_end, opr_end) <= LIMIT, (
        ffn_end, att_end, opr_end, LIMIT)

    Pp = [nc.alloc_psum_tensor(f"P{i}", [128, 1024], F32).ap() for i in range(4)]

    def tquarter(pi, half, f32=False):
        if f32:
            return Pp[pi][:, half * 512:half * 512 + 128]
        return Pp[pi].bitcast(BF16)[:, half * 1024:half * 1024 + 128]

    tq_reg = {}
    fuse_ve = {}

    def transpose_seq(n, src_fn, dst_fn, pi, pre_wait, evac="scalar",
                      fuse_gb=None, in_f32=False):
        ev_sem = SE if (evac == "scalar" or fuse_gb is not None) else VE
        last = {}
        for i in range(n):
            q = tquarter(pi, i % 2, f32=in_f32)
            w = list(pre_wait) if i == 0 else []
            if i >= 2:
                w.append(last[i - 2])
            elif (pi, i % 2) in tq_reg:
                w.append(tq_reg[(pi, i % 2)])
            k.op("tensor", lambda t, i=i, q=q: t.transpose(
                q, src_fn(i), identf[:, :] if in_f32 else identb[:, :]),
                waits=w, incs=[(TE, 1)])
            trv = k.val(TE)
            if fuse_gb is not None:
                ga, ba = fuse_gb
                slot = scr_b[:, (i % 2) * 128:(i % 2) * 128 + 128]
                w2 = [(TE, trv)]
                if i >= 2:
                    w2.append(fuse_ve[i - 2])
                k.op("scalar", lambda s, q=q, slot=slot: s.copy(
                    out=slot, in_=q), waits=w2, incs=[(SE, 1)])
                last[i] = (SE, k.val(SE))
                def fuse_op(v, i=i, slot=slot):
                    v.tensor_scalar(out=dst_fn(i), in0=slot,
                                    scalar1=ga[:, i:i + 1], scalar2=None,
                                    op0=ALU.mult)
                    v.drain()
                    return v.tensor_scalar(out=dst_fn(i), in0=dst_fn(i),
                                           scalar1=ba[:, i:i + 1], scalar2=None,
                                           op0=ALU.add)
                k.op("vector", fuse_op, waits=[(SE, k.val(SE))], incs=[(VE, 1)])
                fuse_ve[i] = (VE, k.val(VE))
                done = fuse_ve[i]
            elif evac == "scalar":
                k.op("scalar", lambda s, i=i, q=q: s.copy(out=dst_fn(i), in_=q),
                     waits=[(TE, trv)], incs=[(SE, 1)])
                last[i] = (SE, k.val(SE))
                done = last[i]
            else:
                k.op("vector", lambda v, i=i, q=q: v.tensor_copy(dst_fn(i), q),
                     waits=[(TE, trv)], incs=[(VE, 1)])
                last[i] = (VE, k.val(VE))
                done = last[i]
            tq_reg[(pi, i % 2)] = last[i]
        return done

    def ln_stats(xtile, par, war=(), scr=None):
        if scr is None:
            scr = scr_f
        c0 = 8 * par
        su, sq = st_a[:, c0:c0 + 1], st_a[:, c0 + 1:c0 + 2]
        mu, var = st_a[:, c0 + 2:c0 + 3], st_a[:, c0 + 3:c0 + 4]
        rstd, t0 = st_a[:, c0 + 4:c0 + 5], st_a[:, c0 + 5:c0 + 6]
        t1 = st_a[:, c0 + 6:c0 + 7]
        k.op("vector", lambda v: (v.reduce_sum(su, xtile[:, :], axis=AXX),
                                  v.drain())[-1],
             waits=war, incs=[(VE, 1)])
        k.op("scalar", lambda s: s.activation(
            out=scr[:, :], in_=xtile[:, :], func=AF.Square, accum_out=sq),
            waits=war, incs=[(SE, 1)])
        sqv = k.val(SE)

        def stats(v):
            v.tensor_scalar(out=mu, in0=su, scalar1=1.0 / D, scalar2=None,
                            op0=ALU.mult)
            v.tensor_scalar(out=var, in0=sq, scalar1=1.0 / D, scalar2=None,
                            op0=ALU.mult)
            v.drain()
            v.scalar_tensor_tensor(out=t0, in0=mu, scalar=-1.0, in1=mu,
                                   op0=ALU.mult, op1=ALU.mult)
            v.drain()
            v.tensor_tensor(out=var, in0=var, in1=t0, op=ALU.add)
            v.drain()
            v.tensor_scalar(out=var, in0=var, scalar1=float(EPS),
                            scalar2=None, op0=ALU.add)
            return v.drain()
        k.op("vector", stats, waits=[(SE, sqv)], incs=[(VE, 1)])
        k.op("scalar", lambda s: s.activation(out=t0, in_=var, func=AF.Ln),
             waits=[(VE, k.val(VE))], incs=[(SE, 1)])
        k.op("scalar", lambda s: (s.activation(out=t1, in_=t0, func=AF.Exp,
                                               scale=-0.5), s.drain())[-1],
             waits=[(SE, k.val(SE))], incs=[(SE, 1)])

        def fin(v):
            v.tensor_tensor(out=t0, in0=t1, in1=t1, op=ALU.mult)
            v.drain()
            v.tensor_tensor(out=t0, in0=t0, in1=var, op=ALU.mult)
            v.drain()
            v.tensor_scalar(out=t0, in0=t0, scalar1=-0.5, scalar2=1.5,
                            op0=ALU.mult, op1=ALU.add)
            v.drain()
            v.tensor_tensor(out=rstd, in0=t1, in1=t0, op=ALU.mult)
            return v.drain()
        k.op("vector", fin, waits=[(SE, k.val(SE))], incs=[(VE, 1)])
        return mu, rstd, k.val(VE), sqv

    def setup():
        onesf = scr_f[:, 1024:1152]
        k.op("gpsimd", lambda g: g.memset(onesf, 1.0))
        k.op("gpsimd", lambda g: g.affine_select(
            out=identf[:, :], in_=onesf, pattern=[[1, 128]],
            compare_op=ALU.is_equal, fill=0.0, channel_multiplier=-1))
        k.op("gpsimd", lambda g: g.tensor_copy(identb[:, :], identf[:, :]))
        k.op("gpsimd", lambda g: g.affine_select(
            out=scr_f[:, 0:128], in_=onesf, pattern=[[1, 128]], base=-64,
            compare_op=ALU.is_equal, fill=0.0, channel_multiplier=-1))
        k.op("gpsimd", lambda g: g.affine_select(
            out=scr_f[:, 128:256], in_=onesf, pattern=[[1, 128]], base=64,
            compare_op=ALU.is_equal, fill=0.0, channel_multiplier=-1),
            incs=[(GE, 1)])
        k.op("vector", lambda v: v.tensor_sub(
            rotT[:, :], scr_f[:, 0:128], scr_f[:, 128:256]),
            waits=[(GE, k.val(GE))], incs=[(VE, 1)])
        k.op("gpsimd", lambda g: g.memset(scr_f[:, 256:384], NEG),
             waits=[(VE, k.val(VE))], incs=[(GE, 1)])
        k.op("gpsimd", lambda g: g.affine_select(
            out=maskA[:, :], in_=scr_f[:, 256:384], pattern=[[1, 128]],
            compare_op=ALU.is_gt, fill=0.0, channel_multiplier=-1))
        k.op("gpsimd", lambda g: g.affine_select(
            out=maskB[:, :], in_=scr_f[:, 256:384], pattern=[[-1, 128]],
            compare_op=ALU.is_gt, fill=0.0, channel_multiplier=1))
        k.op("gpsimd", lambda g: g.memset(ones1[:, :], 1.0))
        for dst, nm in ((g_t, "ln_g"), (b_t, "ln_b"),
                        (bias_t["bq"], "bq"), (bias_t["bk"], "bk"),
                        (bias_t["bv"], "bv"), (ds_t, "down_s"),
                        (gs_t, "gate_s"), (us_t, "up_s")):
            k.op("sync", lambda s, dst=dst, nm=nm: s.dma_start(
                out=dst[:, :],
                in_=VHBM[nm].rearrange("(c p) -> p c", p=128),
                allow_slow_non_contiguous=True),
                incs=[(DSY, 16)])
        k.op("gpsimd", lambda g: g.dma_start(
            out=bo_row[:, :], in_=VHBM["bo"][None, :]), incs=[(DGP, 16)])
        k.barrier()

    def rope_tables():
        th = st_a[:, 0:1]
        thm = st_a[:, 1:2]
        k.op("gpsimd", lambda g: g.iota(
            th, pattern=[[0, 1]], base=0, channel_multiplier=1,
            allow_small_or_imprecise_dtypes=True), incs=[(GE, 1)])

        def thmod(v):
            v.tensor_scalar(out=thm, in0=th, scalar1=64.0, scalar2=None,
                            op0=ALU.is_ge)
            v.drain()
            v.scalar_tensor_tensor(out=th, in0=thm, scalar=-64.0,
                                   in1=th, op0=ALU.mult, op1=ALU.add)
            return v.drain()
        k.op("vector", thmod, waits=[(GE, k.val(GE))], incs=[(VE, 1)])
        k.op("scalar", lambda s: s.activation(
            out=th, in_=th, func=AF.Exp,
            scale=-float(np.log(10000.0) / 64.0)),
            waits=[(VE, k.val(VE))], incs=[(SE, 1)])
        pos = scr_f[:, 0:1024]
        tmp = scr_f[:, 1024:2048]
        ang = scr_b[:, 0:2048].bitcast(F32)
        for half in range(4):
            k.op("gpsimd", lambda g, half=half: g.iota(
                pos, pattern=[[1, 1024]], base=half * 1024, channel_multiplier=0,
                allow_small_or_imprecise_dtypes=True),
                waits=[(SE, k.val(SE)), (VE, k.val(VE))], incs=[(GE, 1)])
            for tab, shift in ((sin_t, 0.0), (cos_t, float(np.pi / 2))):
                def mk(v, shift=shift):
                    v.tensor_scalar(out=ang, in0=pos, scalar1=th, scalar2=None,
                                    op0=ALU.mult)
                    if shift:
                        v.tensor_scalar(out=ang, in0=ang, scalar1=shift,
                                        scalar2=None, op0=ALU.add)
                    v.tensor_scalar(out=tmp, in0=ang,
                                    scalar1=float(1.0 / (2 * np.pi)),
                                    scalar2=None, op0=ALU.mult)
                    v.tensor_scalar(out=tmp, in0=tmp, scalar1=MAGIC,
                                    scalar2=MAGIC, op0=ALU.add, op1=ALU.subtract)
                    v.scalar_tensor_tensor(
                        out=ang, in0=tmp, scalar=float(-2 * np.pi), in1=ang,
                        op0=ALU.mult, op1=ALU.add)
                    return v.tensor_scalar(
                        out=ang, in0=ang, scalar1=float(np.pi - 1e-6),
                        scalar2=float(-np.pi + 1e-6), op0=ALU.min, op1=ALU.max)
                k.op("vector", mk, waits=[(GE, k.val(GE)), (SE, k.val(SE))],
                     incs=[(VE, 1)])
                k.op("scalar", lambda s, tab=tab, half=half: s.activation(
                    out=tab[:, half * 1024:(half + 1) * 1024], in_=ang,
                    func=AF.Sin), waits=[(VE, k.val(VE))], incs=[(SE, 1)])
        k.barrier()

    def phase_ln(src_hbm, pre, gb):
        fin_at, tr_at, sq_at = {}, {}, {}
        for t in range(NT):
            xa, za = xt[t % 2], ztm[t % 2]
            w = list(pre) if t == 0 else []
            if t >= 2:
                w.append((VE, fin_at[t - 2]))
                w.append((SE, sq_at[t - 2]))
                if "st" in dbg:
                    w.append((DSY, k.val(DSY)))
            k.op("sync", lambda s, t=t, xa=xa: s.dma_start(
                out=xa[:, :], in_=src_hbm[t * 128:(t + 1) * 128, :]),
                waits=w, incs=[(DSY, 16)])
            ld = k.val(DSY)
            war = [(DSY, ld)]
            if t >= 2:
                war.append((VE, fin_at[t - 2]))
            mu, rstd, stats_ve, sq_at[t] = ln_stats(xa, t % 2, war)
            if "st" in dbg and src_hbm is x1_hbm:
                k.op("sync", lambda s, t=t, par=t % 2: s.dma_start(
                    out=dbg["st"][t], in_=st_a[:, 8 * par:8 * par + 8]),
                    waits=[(VE, stats_ve)], incs=[(DSY, 16)])
            zw = [(VE, stats_ve)]
            if t >= 2:
                zw.append((TE, tr_at[t - 2]))
            k.op("vector", lambda v, xa=xa, za=za, mu=mu, rstd=rstd:
                 v.tensor_scalar(out=za[:, :], in0=xa[:, :], scalar1=mu,
                                 scalar2=rstd, op0=ALU.subtract, op1=ALU.mult),
                 waits=zw, incs=[(VE, 1)])
            fin_at[t] = k.val(VE)
            transpose_seq(
                NDM, lambda c, za=za: za[:, c * 128:(c + 1) * 128],
                lambda c, t=t: zfm[:, c * TC + t * 128:c * TC + (t + 1) * 128],
                3, [(VE, fin_at[t])], evac="scalar", fuse_gb=gb)
            tr_at[t] = k.val(TE)
        k.barrier()

    wstate = {"rr": 0, "nb_free": {}}

    def stream_wslab(dst, w_hbm, rows, scale_col, first, cache_ap, dst_free,
                     trq, trq_wait):
        if not first:
            k.op("sync", lambda s: s.dma_start(out=dst[:, :], in_=cache_ap),
                 waits=dst_free, incs=[(DSY, 16)])
            return [(DSY, k.val(DSY))]
        nb = wnatb[wstate["rr"] % 4]
        wstate["rr"] += 1
        w = list(wstate["nb_free"].get(id(nb), []))
        k.op("gpsimd", lambda g: g.dma_start(out=nb[:, :], in_=w_hbm[rows, :]),
             waits=w, incs=[(DGP, 16)])
        ldv = k.val(DGP)
        if scale_col is not None:
            k.op("vector", lambda v: (v.tensor_scalar(
                out=nb[:, :], in0=nb[:, :], scalar1=scale_col, scalar2=None,
                op0=ALU.mult), v.drain())[-1], waits=[(DGP, ldv)],
                incs=[(VE, 1)])
            ready = [(VE, k.val(VE))]
        else:
            ready = [(DGP, ldv)]
        for i in range(NDM):
            w = (ready + list(dst_free) + list(trq_wait)) if i == 0 else []
            k.op("scalar", lambda e, i=i: e.dma_start(
                out=dst[:, i * 128:(i + 1) * 128],
                in_=nb[:, i * 128:(i + 1) * 128], transpose=True),
                waits=w, incs=[(DSC, 16)])
        done2 = [(DSC, k.val(DSC))]
        wstate["nb_free"][id(nb)] = done2
        if cache_ap is not None:
            k.op("sync", lambda s: s.dma_start(out=cache_ap, in_=dst[:, :]),
                 waits=done2, incs=[(DSY, 16)])
        return done2

    def stream_cslab(dst, w_hbm, ic, scale_t, first, cache_ap, dst_free, trq,
                     trq_wait):
        if not first:
            k.op("sync", lambda s: s.dma_start(out=dst[:, :], in_=cache_ap),
                 waits=dst_free, incs=[(DSY, 16)])
            return [(DSY, k.val(DSY))]
        nb = wnatb[wstate["rr"] % 4]
        wstate["rr"] += 1
        w = list(wstate["nb_free"].get(id(nb), []))
        k.op("gpsimd", lambda g, ic=ic, nb=nb: g.dma_start(
            out=nb.rearrange("p (c i) -> p c i", c=NDM),
            in_=w_hbm.rearrange(
                "(c p) (i r) -> p c i r", p=128, r=128)[:, :, ic, :]),
            waits=w, incs=[(DGP, 16)])
        ldv = k.val(DGP)
        if scale_t is not None:
            def sc(v, nb=nb):
                inst = None
                for c in range(NDM):
                    inst = v.tensor_scalar(
                        out=nb[:, c * 128:(c + 1) * 128],
                        in0=nb[:, c * 128:(c + 1) * 128],
                        scalar1=scale_t[:, c:c + 1], scalar2=None, op0=ALU.mult)
                return inst
            def sc2(v):
                sc(v)
                return v.drain()
            k.op("vector", sc2, waits=[(DGP, ldv)], incs=[(VE, 1)])
            ready = [(VE, k.val(VE))]
        else:
            ready = [(DGP, ldv)]
        for i in range(NDM):
            w = (ready + list(dst_free) + list(trq_wait)) if i == 0 else []
            k.op("scalar", lambda e, i=i: e.dma_start(
                out=dst[:, i * 128:(i + 1) * 128],
                in_=nb[:, i * 128:(i + 1) * 128], transpose=True),
                waits=w, incs=[(DSC, 16)])
        done2 = [(DSC, k.val(DSC))]
        wstate["nb_free"][id(nb)] = done2
        if cache_ap is not None:
            k.op("sync", lambda s, dst=dst, cache_ap=cache_ap: s.dma_start(
                out=cache_ap, in_=dst[:, :]), waits=done2, incs=[(DSY, 16)])
        return done2

    def phase_ffn(first, res_src_hbm, res_dst_hbm):
        k.wait("sync", [(DSY, k.val(DSY)), (DGP, k.val(DGP))])
        ic0 = 0
        for third, nics in enumerate(QUARTERS):
            ics = list(range(ic0, ic0 + nics))
            ic0 += nics
            h_done = {}
            slab_free = {i: () for i in range(4)}
            for j, ic in enumerate(ics):
                gT, uT = wT[2 * (j % 2)], wT[2 * (j % 2) + 1]
                trw = [(VE, h_done.get((j - 1, 0), 0))]
                wg = stream_wslab(gT, WHBM["gate_w"],
                                  slice(ic * 128, (ic + 1) * 128),
                                  gs_t[:, ic:ic + 1], first, cache["gate_w"][ic],
                                  slab_free[2 * (j % 2)], 1, trw)
                wu = stream_wslab(uT, WHBM["up_w"],
                                  slice(ic * 128, (ic + 1) * 128),
                                  us_t[:, ic:ic + 1], first, cache["up_w"][ic],
                                  slab_free[2 * (j % 2) + 1], 1, trw)
                for th_ in range(2):
                    pg, pu = Pp[th_ * 2], Pp[th_ * 2 + 1]
                    w = list(wg) + list(wu)
                    if (j - 1, th_) in h_done:
                        w.append((VE, h_done[(j - 1, th_)]))

                    def mk(t, gT=gT, uT=uT, th_=th_, pg=pg, pu=pu):
                        inst = None
                        for ps, wt in ((pg, gT), (pu, uT)):
                            for c in range(NDM):
                                for ns in range(2):
                                    off = c * TC + th_ * 1024 + ns * 512
                                    inst = t.matmul(
                                        ps[:, ns * 512:(ns + 1) * 512],
                                        wt[:, c * 128:(c + 1) * 128],
                                        zfm[:, off:off + 512],
                                        start=(c == 0), stop=(c == NDM - 1))
                        return inst
                    k.op("tensor", mk, waits=w, incs=[(TE, 1)])
                    mmv = k.val(TE)
                    k.op("scalar", lambda s, th_=th_, pg=pg: s.activation(
                        out=silu_b[:, th_ * 1024:(th_ + 1) * 1024], in_=pg[:, :],
                        func=AF.Silu), waits=[(TE, mmv)], incs=[(SE, 1)])
                    k.op("vector", lambda v, j=j, th_=th_, pu=pu:
                         v.tensor_tensor(
                             out=h_ar[:, j * TC + th_ * 1024:
                                      j * TC + (th_ + 1) * 1024],
                             in0=pu[:, :],
                             in1=silu_b[:, th_ * 1024:(th_ + 1) * 1024],
                             op=ALU.mult),
                         waits=[(SE, k.val(SE)), (TE, mmv)], incs=[(VE, 1)])
                    h_done[(j, th_)] = k.val(VE)
                slab_free[2 * (j % 2)] = [(TE, k.val(TE))]
                slab_free[2 * (j % 2) + 1] = [(TE, k.val(TE))]
            lastg = [(VE, h_done[(len(ics) - 1, 0)]),
                     (VE, h_done[(len(ics) - 1, 1)])]
            if third == 0 and "h0" in dbg:
                k.op("sync", lambda s: s.dma_start(
                    out=dbg["h0"].transpose([1, 0, 2]),
                    in_=h_ar.rearrange("p (j t) -> p j t", j=11)),
                    waits=[(VE, k.val(VE))], incs=[(DSY, 16)])
            dsl_ready = []
            for j, ic in enumerate(ics):
                trw = lastg if j == 0 else []
                dslf = [(TE, k.val(TE)), (SE, k.val(SE))] if not first else []
                dsl_ready.append(stream_cslab(
                    dsl[j], WHBM["down_w"], ic, ds_t, first,
                    cache["down_w"][ic], dslf, 3, trw))
            stage_done = [(DSY, k.val(DSY)), (DSC, k.val(DSC))]
            ev_at = {0: 0, 1: 0}
            for tg in range(NT):
                par = tg % 2
                pa, pb = Pp[par * 2], Pp[par * 2 + 1]
                w = [x for r in dsl_ready for x in r] + stage_done \
                    if tg == 0 else []
                if tg == 1:
                    w = list(stage_done)
                if tg >= 2:
                    w.append((VE, ev_at[par]))

                def mk(t, tg=tg, pa=pa, pb=pb, nics=nics):
                    inst = None
                    for j in range(nics):
                        hT = h_ar[:, j * TC + tg * 128:j * TC + (tg + 1) * 128]
                        for ns in range(4):
                            ps = pa if ns < 2 else pb
                            inst = t.matmul(
                                ps[:, (ns % 2) * 512:(ns % 2) * 512 + 512],
                                hT, dsl[j][:, ns * 512:(ns + 1) * 512],
                                start=(j == 0), stop=(j == nics - 1))
                    return inst
                k.op("tensor", mk, waits=w, incs=[(TE, 1)])
                mmv = k.val(TE)
                ra = rt[par]
                rows = slice(tg * 128, (tg + 1) * 128)
                if third == 0:
                    xa = xt[par]
                    k.op("sync", lambda s, rows=rows, xa=xa: s.dma_start(
                        out=xa[:, :], in_=res_src_hbm[rows, :]),
                        waits=[(VE, ev_at[par])] if tg >= 2 else [],
                        incs=[(DSY, 16)])
                    ldx = k.val(DSY)

                    def ev(v, xa=xa, ra=ra, pa=pa, pb=pb):
                        v.scalar_tensor_tensor(
                            out=ra[:, 0:1024], in0=pa[:, :], scalar=0.5,
                            in1=xa[:, 0:1024], op0=ALU.mult, op1=ALU.add)
                        return v.scalar_tensor_tensor(
                            out=ra[:, 1024:2048], in0=pb[:, :], scalar=0.5,
                            in1=xa[:, 1024:2048], op0=ALU.mult, op1=ALU.add)
                    k.op("vector", ev, waits=[(TE, mmv), (DSY, ldx)],
                         incs=[(VE, 1)])
                else:
                    def ev(v, ra=ra, pa=pa, pb=pb):
                        v.tensor_scalar(out=ra[:, 0:1024], in0=pa[:, :],
                                        scalar1=0.5, scalar2=None, op0=ALU.mult)
                        return v.tensor_scalar(out=ra[:, 1024:2048],
                                               in0=pb[:, :], scalar1=0.5,
                                               scalar2=None, op0=ALU.mult)
                    wv = [(TE, mmv)]
                    if tg >= 2:
                        wv.append((VE, ev_at[par]))
                    k.op("vector", ev, waits=wv, incs=[(VE, 1)])
                ev_at[par] = k.val(VE)
                if third == 0:
                    k.op("sync", lambda s, ra=ra, rows=rows: s.dma_start(
                        out=res_dst_hbm[rows, :], in_=ra[:, :]),
                        waits=[(VE, ev_at[par])], incs=[(DSY, 16)])
                else:
                    k.op("gpsimd", lambda g, ra=ra, rows=rows: g.dma_start(
                        out=res_dst_hbm[rows, :], in_=ra[:, :],
                        accum_op=ALU.add),
                        waits=[(VE, ev_at[par])], incs=[(DGP, 16)])
            k.barrier()
        k.barrier()

    qkv_store_done = {}

    def phase_qkv():
        k.wait("sync", [(DSY, k.val(DSY)), (DGP, k.val(DGP))])
        pfree = {0: 0, 1: 0, 2: 0}
        sfree = {0: 0, 1: 0}
        slab_free = {i: () for i in range(4)}
        it = 0
        for ip, wnm in enumerate(("wq", "wk", "wv")):
            bcol = bias_t[("bq", "bk", "bv")[ip]]
            for oc in range(NH):
                dst = qwT[it % 4]
                wready = stream_wslab(dst, WHBM[wnm],
                                      slice(oc * 128, (oc + 1) * 128), None,
                                      True, None, slab_free[it % 4], 3, ())
                for th_ in range(2):
                    pq = Pp[(2 * it + th_) % 3]

                    def mk(t, dst=dst, th_=th_, pq=pq):
                        inst = None
                        for c in range(NDM):
                            for ns in range(2):
                                off = c * TC + th_ * 1024 + ns * 512
                                inst = t.matmul(
                                    pq[:, ns * 512:(ns + 1) * 512],
                                    dst[:, c * 128:(c + 1) * 128],
                                    zfm[:, off:off + 512],
                                    start=(c == 0), stop=(c == NDM - 1))
                        return inst
                    w = list(wready)
                    if pfree[(2 * it + th_) % 3]:
                        w.append((VE, pfree[(2 * it + th_) % 3]))
                    k.op("tensor", mk, waits=w, incs=[(TE, 1)])
                    mmv = k.val(TE)
                    sb = qst[th_]
                    wv = [(TE, mmv)]
                    if sfree[th_]:
                        wv.append((DSC, sfree[th_]))
                    k.op("vector", lambda v, sb=sb, pq=pq, bcol=bcol, oc=oc:
                         v.tensor_scalar(out=sb[:, :], in0=pq[:, :],
                                         scalar1=bcol[:, oc:oc + 1],
                                         scalar2=None, op0=ALU.add),
                         waits=wv, incs=[(VE, 1)])
                    pfree[(2 * it + th_) % 3] = k.val(VE)
                    k.op("scalar", lambda s, sb=sb, ip=ip, oc=oc, th_=th_:
                         s.dma_start(
                             out=qkv_send[ip, oc // 2, oc % 2, :,
                                          th_ * 1024:(th_ + 1) * 1024],
                             in_=sb[:, :]),
                         waits=[(VE, k.val(VE))], incs=[(DSC, 16)])
                    sfree[th_] = k.val(DSC)
                slab_free[it % 4] = [(TE, k.val(TE))]
                it += 1
            qkv_store_done[ip] = k.val(DSC)
        k.barrier()

    def phase_exchange(send_ap, recv_ap, wait=None):
        if wait is None:
            wait = [(DSC, k.val(DSC)), (DSY, k.val(DSY))]
        k.wait("gpsimd", wait)

        def f(g):
            with allow_ici_all_to_all():
                return g.collective_compute(
                    "AllToAll", ALU.bypass,
                    replica_groups=[list(range(NCORES))],
                    ins=[send_ap.opt()], outs=[recv_ap.opt()])
        k.op("gpsimd", f, incs=[(CC, 1)])

    def phase_attention():
        k.wait("sync", [(CC, k.val(CC))])
        st8 = {"qkv_free": {0: 0, 1: 0},
               "post_a": 0,
               "post_b": 0,
               "po_evac": {0: 0, 1: 0},
               "pt_guard": 0}

        def emit_pro_a(u):
            b_, hh = u // 2, u % 2
            par = u % 2
            aq, ak, avt = at_q[par], at_k[par], at_vt[par]
            thunks = []
            tfree = st8["qkv_free"][par]
            vfree = st8.get("vt_te", 0)
            for dst, ip in ((aq, 0), (ak, 1), (at_v, 2)):
                for hb in range(2):
                    def ld(dst=dst, ip=ip, hb=hb, b_=b_, hh=hh, tfree=tfree,
                           vfree=vfree):
                        w = [(TE, max(tfree, vfree) if ip == 2 else tfree)]
                        k.op("sync", lambda s: s.dma_start(
                            out=dst[:, hb * TC:(hb + 1) * TC],
                            in_=qkv_recv[ip, 2 * b_ + hb, hh, :, :]),
                            waits=w, incs=[(DSY, 16)])
                        st8["ldv"] = k.val(DSY)
                    thunks.append(ld)
            def mk_rope(ti, sl):
                tens = aq if ti == 0 else ak
                def f(tens=tens, sl=sl, ti=ti):
                    cs = slice(sl * 512, (sl + 1) * 512)
                    pq = Pp[0][:, (sl % 2) * 512:(sl % 2) * 512 + 512]
                    w = [(DSY, st8["ldv"]), (SE, st8["post_a"]),
                         (SE, st8["post_b"])] if (ti == 0 and sl == 0) else []
                    if st8.get("rve2") and (ti, sl) not in ((0, 0), (0, 1)):
                        pr = (ti, sl - 2) if sl >= 2 else (ti - 1, 6 + sl)
                        if pr in st8["rve2"]:
                            w.append((VE, st8["rve2"][pr]))
                    k.op("tensor", lambda t, tens=tens, cs=cs, pq=pq: t.matmul(
                        pq, rotT[:, :], tens[:, cs], start=True, stop=True),
                        waits=w, incs=[(TE, 1)])
                    trv = k.val(TE)
                    def rv(v, tens=tens, cs=cs, pq=pq):
                        v.tensor_tensor(out=at_rt[:, :], in0=pq,
                                        in1=sin_t[:, cs], op=ALU.mult)
                        v.tensor_tensor(out=tens[:, cs], in0=tens[:, cs],
                                        in1=cos_t[:, cs], op=ALU.mult)
                        return v.tensor_tensor(out=tens[:, cs], in0=tens[:, cs],
                                               in1=at_rt[:, :], op=ALU.add)
                    k.op("vector", rv, waits=[(DSY, st8["ldv"]), (TE, trv)],
                         incs=[(VE, 1)])
                    st8["rve2"][(ti, sl)] = k.val(VE)
                return f
            st8["rve2"] = {}
            for ti in range(2):
                for sl in range(8):
                    thunks.append(mk_rope(ti, sl))
            vstate = {}
            def mk_vt(i):
                def f(i=i):
                    q = Pp[1].bitcast(BF16)[:, (i % 2) * 1024:(i % 2) * 1024 + 128]
                    w = [(DSY, st8["ldv"]), (SE, st8["post_a"]),
                         (SE, st8["post_b"])] if i == 0 else []
                    if i >= 2:
                        w.append((SE, vstate[i - 2]))
                    k.op("tensor", lambda t, i=i, q=q: t.transpose(
                        q, at_v[:, i * 128:(i + 1) * 128], identb[:, :]),
                        waits=w, incs=[(TE, 1)])
                    k.op("scalar", lambda s, i=i, q=q: s.copy(
                        out=avt[:, i * 128:(i + 1) * 128], in_=q),
                        waits=[(TE, k.val(TE))], incs=[(SE, 1)])
                    vstate[i] = k.val(SE)
                    st8["vt_se"] = k.val(SE)
                    st8["vt_te"] = k.val(TE)
                return f
            for i in range(32):
                thunks.append(mk_vt(i))
            astate = {"exp": {0: None, 1: None}, "vtiny": 0, "abuf": 0}
            def mk_a(j, ci, k0, w_, nch):
                def f(j=j, ci=ci, k0=k0, w_=w_, nch=nch):
                    par2 = astate["abuf"] % 2
                    astate["abuf"] += 1
                    pa = Pp[par2]
                    def mka(t, j=j, k0=k0, w_=w_, pa=pa):
                        inst = None
                        for off in range(0, w_, 512):
                            n = min(512, w_ - off)
                            inst = t.matmul(
                                pa[:, off:off + n],
                                aq[:, j * 128:(j + 1) * 128],
                                ak[:, k0 + off:k0 + off + n],
                                start=True, stop=True)
                        return inst
                    if astate["exp"][par2] is None:
                        w2 = [(VE, st8["rve2"][(1, 7)]), (SE, st8["vt_se"])]
                    else:
                        w2 = [(SE, astate["exp"][par2])]
                    k.op("tensor", mka, waits=w2, incs=[(TE, 1)])
                    mmv = k.val(TE)
                    wmask = [(TE, mmv)]
                    if k0 + w_ == (j + 1) * 128:
                        dcol = w_ - 128
                        k.op("vector", lambda v, pa=pa, dcol=dcol:
                             v.tensor_tensor(out=pa[:, dcol:dcol + 128],
                                             in0=pa[:, dcol:dcol + 128],
                                             in1=maskA[:, :], op=ALU.add),
                             waits=[(TE, mmv), (VE, astate["vtiny"])],
                             incs=[(VE, 1)])
                        wmask = [(VE, k.val(VE))]
                    cm = at_t[:, 4:5]
                    nbc = at_t[:, 6:7]
                    csum = at_t[:, 7:8]
                    def redmax(v, pa=pa, w_=w_, cm=cm, j=j, first=(ci == 0)):
                        if not first:
                            v.tensor_copy(at_t[:, 0:1], at_m[:, j:j + 1])
                        v.reduce_max(cm, pa[:, 0:w_], axis=AXX)
                        v.drain()
                        if first:
                            v.tensor_copy(at_m[:, j:j + 1], cm)
                        else:
                            v.tensor_max(at_m[:, j:j + 1], at_m[:, j:j + 1], cm)
                        v.drain()
                        v.tensor_scalar(out=nbc, in0=at_m[:, j:j + 1],
                                        scalar1=-ISQ, scalar2=None,
                                        op0=ALU.mult)
                        return v.drain()
                    k.op("vector", redmax,
                         waits=wmask + [(VE, astate["vtiny"])], incs=[(VE, 1)])
                    astate["vtiny"] = k.val(VE)
                    k.op("scalar", lambda s, pa=pa, w_=w_, nbc=nbc, csum=csum:
                         s.activation(out=aexp_b[:, 0:w_], in_=pa[:, 0:w_],
                                      func=AF.Exp, bias=nbc, scale=ISQ,
                                      accum_out=csum),
                         waits=[(VE, astate["vtiny"])], incs=[(SE, 1)])
                    astate["exp"][par2] = k.val(SE)
                    if ci == 0:
                        k.op("vector", lambda v, j=j, csum=csum: (
                            v.tensor_copy(at_s[:, j:j + 1], csum),
                            v.drain())[-1],
                            waits=[(SE, astate["exp"][par2])], incs=[(VE, 1)])
                    else:
                        def upd(v, j=j):
                            v.tensor_sub(at_t[:, 1:2], at_t[:, 0:1],
                                         at_m[:, j:j + 1])
                            v.drain()
                            v.tensor_scalar(out=at_t[:, 1:2], in0=at_t[:, 1:2],
                                            scalar1=ISQ, scalar2=None,
                                            op0=ALU.mult)
                            return v.drain()
                        k.op("vector", upd, waits=[(SE, astate["exp"][par2])],
                             incs=[(VE, 1)])
                        k.op("scalar", lambda s: (s.activation(
                            out=at_t[:, 1:2], in_=at_t[:, 1:2], func=AF.Exp),
                            s.drain())[-1],
                            waits=[(VE, k.val(VE))], incs=[(SE, 1)])
                        def upd2(v, j=j, csum=csum):
                            v.tensor_tensor(out=at_s[:, j:j + 1],
                                            in0=at_s[:, j:j + 1],
                                            in1=at_t[:, 1:2], op=ALU.mult)
                            v.drain()
                            v.tensor_tensor(out=at_s[:, j:j + 1],
                                            in0=at_s[:, j:j + 1], in1=csum,
                                            op=ALU.add)
                            return v.drain()
                        k.op("vector", upd2, waits=[(SE, k.val(SE))],
                             incs=[(VE, 1)])
                    astate["vtiny"] = k.val(VE)
                    st8["post_a"] = max(x for x in astate["exp"].values() if x)
                return f
            for j in range(32):
                end = (j + 1) * 128
                chunks = [(s0, min(1024, end - s0))
                          for s0 in range(0, end, 1024)]
                for ci, (k0, w_) in enumerate(chunks):
                    thunks.append(mk_a(j, ci, k0, w_, len(chunks)))
            return thunks

        def emit_nm(u):
            def nmrow(v):
                v.tensor_copy(at_mt[:, 0:32], at_s[:, 0:32])
                return v.drain()
            k.op("vector", nmrow, waits=[(VE, k.val(VE)), (SE, k.val(SE))],
                 incs=[(VE, 1)])
            k.op("scalar", lambda s: (s.activation(
                out=at_mt[:, 0:32], in_=at_mt[:, 0:32], func=AF.Ln),
                s.drain())[-1], waits=[(VE, k.val(VE))], incs=[(SE, 1)])
            def nmrow2(v):
                v.scalar_tensor_tensor(out=at_mt[:, 0:32], in0=at_mt[:, 0:32],
                                       scalar=SQ128, in1=at_m[:, 0:32],
                                       op0=ALU.mult, op1=ALU.add)
                v.drain()
                v.tensor_scalar(out=at_mt[:, 0:32], in0=at_mt[:, 0:32],
                                scalar1=-1.0, scalar2=None, op0=ALU.mult)
                v.drain()
                v.tensor_copy(at_hi[:, 0:32], at_mt[:, 0:32])
                v.drain()
                v.tensor_tensor(out=at_mt[:, 0:32], in0=at_mt[:, 0:32],
                                in1=at_hi[:, 0:32], op=ALU.subtract)
                v.drain()
                v.tensor_copy(at_lo[:, 0:32], at_mt[:, 0:32])
                return v.drain()
            k.op("vector", nmrow2, waits=[(SE, k.val(SE))], incs=[(VE, 1)])
            nm_ve = k.val(VE)
            for hi_lo, (srct, cols) in enumerate(((at_hi, slice(0, 128)),
                                                  (at_lo, slice(128, 256)))):
                q = Pp[2].bitcast(BF16)[:, hi_lo * 1024:hi_lo * 1024 + 128]
                wnm = [(VE, nm_ve), (SE, st8["post_a"]), (SE, st8["post_b"])] \
                    if hi_lo == 0 else []
                k.op("tensor", lambda t, srct=srct, q=q: t.transpose(
                    q[0:32, :], srct[:, 0:32], identb[:, :]),
                    waits=wnm, incs=[(TE, 1)])
                k.op("scalar", lambda s, cols=cols, q=q: s.copy(
                    out=at_nmt[:, cols], in_=q[0:32, :]),
                    waits=[(TE, k.val(TE))], incs=[(SE, 1)])
            nm_se = k.val(SE)
            k.op("scalar", lambda s: s.dma_start(
                out=nm_bounce.transpose([1, 0, 2]),
                in_=at_nmt.rearrange("q (h i) -> q h i", h=2)),
                waits=[(SE, nm_se)], incs=[(DSC, 16)])
            k.op("sync", lambda s: s.dma_start(
                out=at_nmh[:, :],
                in_=nm_bounce[0].rearrange("q i -> (q i)")[None, :]),
                waits=[(DSC, k.val(DSC)), (TE, k.val(TE))], incs=[(DSY, 16)])
            k.op("sync", lambda s: s.dma_start(
                out=at_nml[:, :],
                in_=nm_bounce[1].rearrange("q i -> (q i)")[None, :]),
                incs=[(DSY, 16)])
            return (DSY, k.val(DSY))

        def emit_bpv(u, nm_ready):
            b_, hh = u // 2, u % 2
            par = u % 2
            aq, ak, avt = at_q[par], at_k[par], at_vt[par]
            bstate = {"pt": {0: None, 1: None}, "exp": {0: None, 1: None},
                      "bbuf": 0}
            thunks = []
            def mk_b(g8, kc, nkc):
                def f(g8=g8, kc=kc, nkc=nkc):
                    po = Pp[3][:, (g8 % 2) * 512:(g8 % 2) * 512 + 512]
                    par2 = bstate["bbuf"] % 2
                    bstate["bbuf"] += 1
                    pb = Pp[2][:, par2 * 512:par2 * 512 + 512]
                    def mkb(t, kc=kc, g8=g8, pb=pb):
                        t.matmul(pb, ak[:, kc * 128:(kc + 1) * 128],
                                 aq[:, g8 * 512:(g8 + 1) * 512],
                                 start=True, stop=False)
                        t.matmul(pb, ones1[0:1, 0:128],
                                 at_nmh[0:1, g8 * 512:(g8 + 1) * 512],
                                 start=False, stop=False)
                        return t.matmul(pb, ones1[0:1, 0:128],
                                        at_nml[0:1, g8 * 512:(g8 + 1) * 512],
                                        start=False, stop=True)
                    if bstate["exp"][par2] is None:
                        w = [nm_ready, (SE, st8["post_a"])]
                    else:
                        w = [(SE, bstate["exp"][par2])]
                    k.op("tensor", mkb, waits=w, incs=[(TE, 1)])
                    mmv = k.val(TE)
                    dq = kc - 4 * g8
                    wexp = [(TE, mmv)]
                    if 0 <= dq < 4:
                        k.op("vector", lambda v, pb=pb, dq=dq: v.tensor_tensor(
                            out=pb[:, dq * 128:(dq + 1) * 128],
                            in0=pb[:, dq * 128:(dq + 1) * 128],
                            in1=maskB[:, :], op=ALU.add),
                            waits=[(TE, mmv)], incs=[(VE, 1)])
                        wexp = [(VE, k.val(VE))]
                    pt = at_p[par2]
                    if bstate["pt"][par2]:
                        wexp = wexp + [bstate["pt"][par2]]
                    k.op("scalar", lambda s, pb=pb, pt=pt: s.activation(
                        out=pt[:, :], in_=pb, func=AF.Exp, scale=ISQ),
                        waits=wexp, incs=[(SE, 1)])
                    bstate["exp"][par2] = k.val(SE)
                    st8["post_b"] = k.val(SE)
                    nz = max(0, min(kc - 4 * g8, 4))
                    if nz > 0:
                        k.op("vector", lambda v, pt=pt, nz=nz: v.memset(
                            pt[:, 0:nz * 128], 0.0),
                            waits=[(SE, bstate["exp"][par2])], incs=[(VE, 1)])
                        ptw = (VE, k.val(VE))
                    else:
                        ptw = (SE, bstate["exp"][par2])
                    wpv = [ptw]
                    if kc == 0 and st8["po_evac"][g8 % 2]:
                        wpv.append((SE, st8["po_evac"][g8 % 2]))
                    k.op("tensor", lambda t, kc=kc, po=po, pt=pt,
                         first=(kc == 0), last=(kc == nkc - 1): t.matmul(
                             po, avt[:, kc * 128:(kc + 1) * 128], pt[:, :],
                             start=first, stop=last),
                         waits=wpv, incs=[(TE, 1)])
                    bstate["pt"][par2] = (TE, k.val(TE))
                    if kc == nkc - 1:
                        ob = at_o[g8 % 2]
                        k.op("scalar", lambda s, ob=ob, po=po: s.copy(
                            out=ob[:, :], in_=po),
                            waits=[(TE, k.val(TE)), (DSC, k.val(DSC))],
                            incs=[(SE, 1)])
                        st8["po_evac"][g8 % 2] = k.val(SE)
                        k.op("scalar", lambda s, ob=ob, g8=g8, b_=b_, hh=hh:
                             s.dma_start(
                                 out=o_send[2 * b_ + g8 // 4, hh, :,
                                            (g8 % 4) * 512:(g8 % 4) * 512 + 512],
                                 in_=ob[:, :]),
                             waits=[(SE, st8["po_evac"][g8 % 2])],
                             incs=[(DSC, 16)])
                return f
            for g8 in range(8):
                nkc = 4 * g8 + 4
                for kc in range(nkc):
                    thunks.append(mk_b(g8, kc, nkc))
            return thunks

        prev_bpv = []
        for u in range(8):
            pa_thunks = emit_pro_a(u)
            na, nb = len(pa_thunks), len(prev_bpv)
            ia = ib = 0
            while ia < na or ib < nb:
                take_b = ib < nb and (ia >= na or ib * na <= ia * nb)
                if take_b:
                    prev_bpv[ib]()
                    ib += 1
                else:
                    pa_thunks[ia]()
                    ia += 1
            nm_ready = emit_nm(u)
            st8["qkv_free"][(u + 1) % 2] = k.val(TE)
            prev_bpv = emit_bpv(u, nm_ready)
        for f in prev_bpv:
            f()
        k.barrier()

    def phase_oproj():
        k.wait("sync", [(CC, k.val(CC))])
        x2w_done = 0
        oproj_sq = {0: 0, 1: 0}
        for half in range(2):
            hbase = 8 * half
            ldw = [(TE, k.val(TE))] if half == 1 else []
            for i in range(8):
                h = hbase + i
                k.op("sync", lambda s, i=i, h=h: s.dma_start(
                    out=ofm[:, i * TC:(i + 1) * TC],
                    in_=o_recv[h // 2, h % 2, :, :]),
                    waits=ldw if i == 0 else [], incs=[(DSY, 16)])
            ldv = k.val(DSY)
            wo_ready = []
            for i in range(8):
                trw = [(VE, k.val(VE))] if (half == 1 and i == 0) else ()
                wo_ready.append(stream_cslab(
                    wsl[i], WHBM["wo"], hbase + i, None, True, None,
                    [(TE, k.val(TE))] if half == 1 and i < 4 else (), 3, trw))
            stage_done = list(wo_ready[-1])
            ev_at = {0: 0, 1: 0}
            for tg in range(NT):
                par = tg % 2
                pa, pb = Pp[par * 2], Pp[par * 2 + 1]
                w = stage_done + [(DSY, ldv)] if tg <= 1 else []
                if tg >= 2:
                    w.append((VE, ev_at[par]))

                def mk(t, tg=tg, pa=pa, pb=pb, half=half):
                    inst = None
                    for c in range(8):
                        oT = ofm[:, c * TC + tg * 128:c * TC + (tg + 1) * 128]
                        for ns in range(4):
                            ps = pa if ns < 2 else pb
                            inst = t.matmul(
                                ps[:, (ns % 2) * 512:(ns % 2) * 512 + 512], oT,
                                wsl[c][:, ns * 512:(ns + 1) * 512],
                                start=(c == 0),
                                stop=(c == 7 and half == 1))
                    if half == 0:
                        for ns in range(4):
                            ps = pa if ns < 2 else pb
                            inst = t.matmul(
                                ps[:, (ns % 2) * 512:(ns % 2) * 512 + 512],
                                ones1[0:1, 0:128],
                                bo_row[0:1, ns * 512:(ns + 1) * 512],
                                start=False, stop=True)
                    return inst
                k.op("tensor", mk, waits=w, incs=[(TE, 1)])
                mmv = k.val(TE)
                ra = rt[par]
                rows = slice(tg * 128, (tg + 1) * 128)
                if half == 0:
                    xa = xt[par]
                    k.op("sync", lambda s, rows=rows, xa=xa: s.dma_start(
                        out=xa[:, :], in_=x1_hbm[rows, :]),
                        waits=([(VE, ev_at[par]), (SE, oproj_sq[par])]
                               if tg >= 2 else []),
                        incs=[(DSY, 16)])
                    ldx = k.val(DSY)
                    war = [(DSY, ldx)]
                    if tg >= 2:
                        war.append((VE, ev_at[par]))
                    mu, rstd, stats_ve, sq_o = ln_stats(xa, par, war,
                                                        scr=oscr_f)
                    oproj_sq[par] = sq_o

                    def ev(v, xa=xa, ra=ra, pa=pa, pb=pb, mu=mu, rstd=rstd):
                        v.tensor_scalar(out=ra[:, :], in0=xa[:, :], scalar1=mu,
                                        scalar2=rstd, op0=ALU.subtract,
                                        op1=ALU.mult)
                        v.tensor_tensor(out=ra[:, :], in0=ra[:, :],
                                        in1=xa[:, :], op=ALU.add)
                        v.tensor_tensor(out=ra[:, 0:1024], in0=ra[:, 0:1024],
                                        in1=pa[:, :], op=ALU.add)
                        return v.tensor_tensor(out=ra[:, 1024:2048],
                                               in0=ra[:, 1024:2048],
                                               in1=pb[:, :], op=ALU.add)
                    k.op("vector", ev, waits=[(TE, mmv), (VE, stats_ve)],
                         incs=[(VE, 1)])
                    ev_at[par] = k.val(VE)
                    k.op("sync", lambda s, ra=ra, rows=rows: s.dma_start(
                        out=x2_hbm[rows, :], in_=ra[:, :]),
                        waits=[(VE, ev_at[par])], incs=[(DSY, 16)])
                else:
                    def ev(v, ra=ra, pa=pa, pb=pb):
                        v.tensor_copy(ra[:, 0:1024], pa[:, :])
                        return v.tensor_copy(ra[:, 1024:2048], pb[:, :])
                    wv = [(TE, mmv)]
                    if tg >= 2:
                        wv.append((VE, ev_at[par]))
                    k.op("vector", ev, waits=wv, incs=[(VE, 1)])
                    ev_at[par] = k.val(VE)
                    k.op("gpsimd", lambda g, ra=ra, rows=rows: g.dma_start(
                        out=x2_hbm[rows, :], in_=ra[:, :], accum_op=ALU.add),
                        waits=[(VE, ev_at[par]), (DSY, x2w_done)],
                        incs=[(DGP, 16)])
            if half == 0:
                x2w_done = k.val(DSY)
        k.barrier()

    setup()
    if upto >= 1:
        phase_ln(x_in, [(DSY, k.val(DSY))], None)
    if upto >= 2:
        phase_ffn(True, x_in, x1_hbm)
    if upto >= 3:
        phase_ln(x1_hbm, [(DSY, k.val(DSY)), (DGP, k.val(DGP))], (g_t, b_t))
    if upto >= 4:
        phase_qkv()
    if upto >= 5:
        for ip in range(3):
            phase_exchange(qkv_send[ip], qkv_recv[ip])
        rope_tables()
    if upto >= 6:
        phase_attention()
    if upto >= 7:
        phase_exchange(o_send, o_recv)
    if upto >= 8:
        phase_oproj()
    if upto >= 9:
        phase_ln(x2_hbm, [(DSY, k.val(DSY)), (DGP, k.val(DGP))], None)
    if upto >= 10:
        phase_ffn(False, x2_hbm, out)

    if debug:
        def dump(name, src_ap):
            k.op("sync", lambda s, name=name, src_ap=src_ap: s.dma_start(
                out=dbg[name], in_=src_ap),
                waits=[(CC, k.val(CC)), (DGP, k.val(DGP)),
                       (DSC, k.val(DSC))], incs=[(DSY, 16)])
        k.barrier()
        for nm in dbg:
            if nm == "zfm":
                k.op("sync", lambda s: s.dma_start(
                    out=dbg["zfm"].transpose([1, 0, 2]),
                    in_=zfm.rearrange("p (c t) -> p c t", c=NDM)),
                    incs=[(DSY, 16)])
            elif nm == "x1":
                dump("x1", x1_hbm)
            elif nm == "gc":
                dump("gc", cache["gate_w"])
            elif nm == "dc":
                dump("dc", cache["down_w"])
            elif nm == "x2":
                dump("x2", x2_hbm)
            elif nm == "qs":
                dump("qs", qkv_send)
            elif nm == "qr":
                dump("qr", qkv_recv)
            elif nm == "osend":
                dump("osend", o_send)
            elif nm == "orecv":
                dump("orecv", o_recv)
    k.barrier()
    k.wait("sync", [(DSY, k.val(DSY)), (DGP, k.val(DGP)), (DSC, k.val(DSC)),
                    (CC, k.val(CC))])
    k.barrier()

    with nc.Block() as block:
        for eng in ENGINES:
            def reg(e, eng=eng):
                for f in k.q[eng]:
                    f(e)
            getattr(block, eng)(reg)
    return nc


_CACHED = {}


def kernel(**inputs):
    if "nc" not in _CACHED:
        _CACHED["nc"] = build_program()
    nc = _CACHED["nc"]
    x = np.ascontiguousarray(inputs["L0_attn_h"], dtype=np.float32)
    base = {
        "wq": inputs["L0_attn_wq"], "wk": inputs["L0_attn_wk"],
        "wv": inputs["L0_attn_wv"], "wo": inputs["L0_attn_wo"],
        "bq": inputs["L0_attn_bq"], "bk": inputs["L0_attn_bk"],
        "bv": inputs["L0_attn_bv"], "bo": inputs["L0_attn_bo"],
        "ln_g": inputs["L0_attn_ln_g"], "ln_b": inputs["L0_attn_ln_b"],
        "gate_w": inputs["L0_ffn_gate_w"], "up_w": inputs["L0_ffn_up_w"],
        "down_w": inputs["L0_ffn_down_w"], "gate_s": inputs["L0_ffn_gate_s"],
        "up_s": inputs["L0_ffn_up_s"], "down_s": inputs["L0_ffn_down_s"],
    }
    base = {k2: np.ascontiguousarray(v, dtype=np.float32)
            for k2, v in base.items()}
    in_maps = []
    for c in range(NCORES):
        bb, h = c // 2, c % 2
        m = dict(base)
        m["x"] = np.ascontiguousarray(x[bb, h * TC:(h + 1) * TC, :])
        in_maps.append(m)
    res = run_bass_kernel_spmd(nc, in_maps, core_ids=list(range(NCORES)))
    outp = np.empty((B, S, D), dtype=np.float32)
    for c in range(NCORES):
        bb, h = c // 2, c % 2
        outp[bb, h * TC:(h + 1) * TC, :] = res.results[c]["out"]
    return outp


# version 15
# speedup vs baseline: 1.1811x; 1.1811x over previous
import numpy as np
import concourse.bass as bass
import concourse.mybir as mybir
from concourse.bass_utils import run_bass_kernel_spmd
from concourse.aot_env import allow_ici_all_to_all

F32 = mybir.dt.float32
BF16 = mybir.dt.bfloat16
AF = mybir.ActivationFunctionType
ALU = mybir.AluOpType
AXX = mybir.AxisListType.X

B, S, D = 4, 4096, 2048
NCORES = 8
TC = S // 2
NT = TC // 128
NDM = D // 128
INTER = 5632
NIC = INTER // 128
QUARTERS = (11, 11, 11, 11)
NH = 16
EPS = 1e-5
ISQ = float(1.0 / np.sqrt(128.0))
SQ128 = float(np.sqrt(128.0))
MAGIC = 12582912.0
NEG = -30000.0

ENGINES = ("sync", "scalar", "vector", "tensor", "gpsimd")


class KB:
    def __init__(self, nc):
        self.nc = nc
        self.q = {e: [] for e in ENGINES}
        self.sem = {}
        self.cnt = {}
        self.nbar = 0

    def new_sem(self, name):
        self.sem[name] = self.nc.alloc_semaphore(name)
        self.cnt[name] = 0

    def val(self, name):
        return self.cnt[name]

    def op(self, eng, fn, waits=(), incs=()):
        waits = tuple((s, v) for s, v in waits if v > 0)
        for s, n in incs:
            self.cnt[s] += n
        incs = tuple(incs)
        sems = self.sem

        def run(e):
            for s, v in waits:
                e.wait_ge(sems[s], v)
            inst = fn(e)
            for s, n in incs:
                inst = inst.then_inc(sems[s], n)
        self.q[eng].append(run)

    def wait(self, eng, waits):
        waits = tuple((s, v) for s, v in waits if v > 0)
        if not waits:
            return
        sems = self.sem

        def run(e):
            for s, v in waits:
                e.wait_ge(sems[s], v)
        self.q[eng].append(run)

    def barrier(self, engines=ENGINES):
        self.nbar += 1
        name = f"bar{self.nbar}"
        self.new_sem(name)
        n = len(engines)
        sems = self.sem
        for eng in engines:
            def run(e, name=name, n=n):
                e.nop().then_inc(sems[name], 1)
                e.wait_ge(sems[name], n)
            self.q[eng].append(run)
        self.cnt[name] = n


class Arena:
    def __init__(self, nc, base, limit):
        self.nc, self.off, self.limit, self.n = nc, base, limit, 0

    def alloc(self, name, shape, dtype):
        sz = int(np.prod(shape[1:])) * mybir.dt.size(dtype)
        off = (self.off + 31) & ~31
        assert off + sz <= self.limit, \
            f"SBUF overflow at {name}: {off}+{sz}>{self.limit}"
        self.off = off + sz
        self.n += 1
        return self.nc.alloc_sbuf_tensor_at(name, shape, dtype, offset=off).ap()

    def mark(self):
        return self.off

    def reset(self, m):
        self.off = m


def build_program(upto=99, debug=None):
    nc = bass.Bass("TRN2", target_bir_lowering=False, debug=False,
                   detect_race_conditions=False)

    x_in = nc.dram_tensor("x", [TC, D], F32, kind="ExternalInput").ap()
    out = nc.dram_tensor("out", [TC, D], F32, kind="ExternalOutput").ap()
    WHBM = {}
    for nm, shp in (("wq", [D, D]), ("wk", [D, D]), ("wv", [D, D]),
                    ("wo", [D, D]), ("gate_w", [INTER, D]),
                    ("up_w", [INTER, D]), ("down_w", [D, INTER])):
        WHBM[nm] = nc.dram_tensor(nm, shp, F32, kind="ExternalInput").ap()
    VHBM = {}
    for nm, n in (("bq", D), ("bk", D), ("bv", D), ("bo", D), ("ln_g", D),
                  ("ln_b", D), ("gate_s", INTER), ("up_s", INTER),
                  ("down_s", D)):
        VHBM[nm] = nc.dram_tensor(nm, [n], F32, kind="ExternalInput").ap()

    x1_hbm = nc.dram_tensor("x1_hbm", [TC, D], F32).ap()
    x2_hbm = nc.dram_tensor("x2_hbm", [TC, D], F32).ap()
    cache = {nm: nc.dram_tensor(nm + "_c", [NIC, 128, D], BF16).ap()
             for nm in ("gate_w", "up_w", "down_w")}
    qkv_send = nc.dram_tensor("qkv_send", [3, 8, 2, 128, TC], BF16).ap()
    qkv_recv = nc.dram_tensor("qkv_recv", [3, 8, 2, 128, TC], BF16).ap()
    o_send = nc.dram_tensor("o_send", [8, 2, 128, TC], BF16).ap()
    o_recv = nc.dram_tensor("o_recv", [8, 2, 128, TC], BF16).ap()
    nm_bounce = nc.dram_tensor("nm_bounce", [2, 32, 128], BF16).ap()
    dbg = {}
    if debug:
        for nm, (shp, dt_) in debug.items():
            dbg[nm] = nc.dram_tensor("dbg_" + nm, shp, dt_,
                                     kind="ExternalOutput").ap()

    k = KB(nc)
    TE, VE, SE, GE = "s_te", "s_ve", "s_se", "s_ge"
    DSY, DGP, DSC, CC = "d_sy", "d_gp", "d_sc", "cc"
    for s in (TE, VE, SE, GE, DSY, DGP, DSC, CC):
        k.new_sem(s)

    probe = nc.alloc_sbuf_tensor("bprobe", [128, 32], F32)
    base = nc.lookup_mloc(probe).addr + 128
    LIMIT = base + nc.sbuf_bytes_remaining - 64
    A = Arena(nc, base, LIMIT)

    identb = A.alloc("identb", [128, 128], BF16)
    identf = A.alloc("identf", [128, 128], F32)
    rotT = A.alloc("rotT", [128, 128], BF16)
    maskA = A.alloc("maskA", [128, 128], F32)
    maskB = A.alloc("maskB", [128, 128], F32)
    ones1 = A.alloc("ones1", [1, 128], BF16)
    g_t = A.alloc("g_t", [128, NDM], F32)
    b_t = A.alloc("b_t", [128, NDM], F32)
    bias_t = {nm: A.alloc(nm + "_t", [128, NDM], F32)
              for nm in ("bq", "bk", "bv")}
    ds_t = A.alloc("ds_t", [128, NDM], F32)
    gs_t = A.alloc("gs_t", [128, NIC], F32)
    us_t = A.alloc("us_t", [128, NIC], F32)
    bo_row = A.alloc("bo_row", [1, D], BF16)
    st_a = A.alloc("st_a", [128, 16], F32)
    wnatb = [A.alloc(f"wnatb{i}", [128, D], BF16) for i in range(4)]
    zfm = A.alloc("zfm", [128, NDM * TC], BF16)

    ph = A.mark()
    xt = [A.alloc(f"xt{i}", [128, D], F32) for i in range(2)]
    rt = [A.alloc(f"rt{i}", [128, D], F32) for i in range(2)]
    ztm = [A.alloc(f"ztm{i}", [128, D], BF16) for i in range(2)]
    scr_f = A.alloc("scr_f", [128, 2048], F32)
    scr_b = A.alloc("scr_b", [128, 2048], BF16)
    A.reset(ph)
    _xt = [A.alloc(f"xtf{i}", [128, D], F32) for i in range(2)]
    _rt = [A.alloc(f"rtf{i}", [128, D], F32) for i in range(2)]
    ffn_pre = A.mark()
    wT = [A.alloc(f"wT{i}", [128, D], BF16) for i in range(4)]
    silu_b = A.alloc("silu_b", [128, 2048], BF16)
    h_ar = A.alloc("h_ar", [128, 11 * TC], BF16)
    ffn_gu_end = A.mark()
    dsl = []
    A.reset(ffn_pre)
    for j in range(5):
        dsl.append(A.alloc(f"dsl{j}", [128, D], BF16))
    A.reset(ffn_gu_end)
    for j in range(5, 11):
        dsl.append(A.alloc(f"dsl{j}", [128, D], BF16))
    ffn_end = A.mark()
    A.reset(ph)
    qwT = [A.alloc(f"qwT{i}", [128, D], BF16) for i in range(4)]
    qst = [A.alloc(f"qst{i}", [128, 1024], BF16) for i in range(2)]
    A.reset(ph)
    at_q = [A.alloc(f"at_q{i}", [128, S], BF16) for i in range(2)]
    at_k = [A.alloc(f"at_k{i}", [128, S], BF16) for i in range(2)]
    at_v = A.alloc("at_v", [128, S], BF16)
    at_vt = [A.alloc(f"at_vt{i}", [128, S], BF16) for i in range(2)]
    cos_t = A.alloc("cos_t", [128, S], BF16)
    sin_t = A.alloc("sin_t", [128, S], BF16)
    aexp_b = A.alloc("aexp_b", [128, 1024], BF16)
    at_p = [A.alloc(f"at_p{i}", [128, 512], BF16) for i in range(2)]
    at_rt = A.alloc("at_rt", [128, 512], BF16)
    at_o = [A.alloc(f"at_o{i}", [128, 512], BF16) for i in range(2)]
    at_m = A.alloc("at_m", [128, 32], F32)
    at_s = A.alloc("at_s", [128, 32], F32)
    at_t = A.alloc("at_t", [128, 8], F32)
    at_mt = A.alloc("at_mt", [128, 32], F32)
    at_hi = A.alloc("at_hi", [128, 32], BF16)
    at_lo = A.alloc("at_lo", [128, 32], BF16)
    at_nmt = A.alloc("at_nmt", [32, 256], BF16)
    at_nmh = A.alloc("at_nmh", [1, 4096], BF16)
    at_nml = A.alloc("at_nml", [1, 4096], BF16)
    att_end = A.mark()
    A.reset(ph)
    _xo = [A.alloc(f"xto{i}", [128, D], F32) for i in range(2)]
    _ro = [A.alloc(f"rto{i}", [128, D], F32) for i in range(2)]
    ofm = A.alloc("ofm", [128, 8 * TC], BF16)
    wsl = [A.alloc(f"wsl{i}", [128, D], BF16) for i in range(8)]
    oscr_f = A.alloc("oscr_f", [128, 2048], F32)
    opr_end = A.mark()
    A.reset(ph)
    assert max(ffn_end, att_end, opr_end) <= LIMIT, (
        ffn_end, att_end, opr_end, LIMIT)

    Pp = [nc.alloc_psum_tensor(f"P{i}", [128, 1024], F32).ap() for i in range(4)]

    def tquarter(pi, half, f32=False):
        if f32:
            return Pp[pi][:, half * 512:half * 512 + 128]
        return Pp[pi].bitcast(BF16)[:, half * 1024:half * 1024 + 128]

    tq_reg = {}
    fuse_ve = {}

    def transpose_seq(n, src_fn, dst_fn, pi, pre_wait, evac="scalar",
                      fuse_gb=None, in_f32=False):
        ev_sem = SE if (evac == "scalar" or fuse_gb is not None) else VE
        last = {}
        for i in range(n):
            q = tquarter(pi, i % 2, f32=in_f32)
            w = list(pre_wait) if i == 0 else []
            if i >= 2:
                w.append(last[i - 2])
            elif (pi, i % 2) in tq_reg:
                w.append(tq_reg[(pi, i % 2)])
            k.op("tensor", lambda t, i=i, q=q: t.transpose(
                q, src_fn(i), identf[:, :] if in_f32 else identb[:, :]),
                waits=w, incs=[(TE, 1)])
            trv = k.val(TE)
            if fuse_gb is not None:
                ga, ba = fuse_gb
                slot = scr_b[:, (i % 2) * 128:(i % 2) * 128 + 128]
                w2 = [(TE, trv)]
                if i >= 2:
                    w2.append(fuse_ve[i - 2])
                k.op("scalar", lambda s, q=q, slot=slot: s.copy(
                    out=slot, in_=q), waits=w2, incs=[(SE, 1)])
                last[i] = (SE, k.val(SE))
                def fuse_op(v, i=i, slot=slot):
                    v.tensor_scalar(out=dst_fn(i), in0=slot,
                                    scalar1=ga[:, i:i + 1], scalar2=None,
                                    op0=ALU.mult)
                    v.drain()
                    return v.tensor_scalar(out=dst_fn(i), in0=dst_fn(i),
                                           scalar1=ba[:, i:i + 1], scalar2=None,
                                           op0=ALU.add)
                k.op("vector", fuse_op, waits=[(SE, k.val(SE))], incs=[(VE, 1)])
                fuse_ve[i] = (VE, k.val(VE))
                done = fuse_ve[i]
            elif evac == "scalar":
                k.op("scalar", lambda s, i=i, q=q, fl=(i >= n - 2): (
                    s.copy(out=dst_fn(i), in_=q),
                    s.drain() if fl else None)[0 if not fl else 1],
                     waits=[(TE, trv)], incs=[(SE, 1)])
                last[i] = (SE, k.val(SE))
                done = last[i]
            else:
                k.op("vector", lambda v, i=i, q=q: v.tensor_copy(dst_fn(i), q),
                     waits=[(TE, trv)], incs=[(VE, 1)])
                last[i] = (VE, k.val(VE))
                done = last[i]
            tq_reg[(pi, i % 2)] = last[i]
        return done

    def ln_stats(xtile, par, war=(), scr=None):
        if scr is None:
            scr = scr_f
        c0 = 8 * par
        su, sq = st_a[:, c0:c0 + 1], st_a[:, c0 + 1:c0 + 2]
        mu, var = st_a[:, c0 + 2:c0 + 3], st_a[:, c0 + 3:c0 + 4]
        rstd, t0 = st_a[:, c0 + 4:c0 + 5], st_a[:, c0 + 5:c0 + 6]
        t1 = st_a[:, c0 + 6:c0 + 7]
        k.op("vector", lambda v: (v.reduce_sum(su, xtile[:, :], axis=AXX),
                                  v.drain())[-1],
             waits=war, incs=[(VE, 1)])
        k.op("scalar", lambda s: s.activation(
            out=scr[:, :], in_=xtile[:, :], func=AF.Square, accum_out=sq),
            waits=war, incs=[(SE, 1)])
        sqv = k.val(SE)

        def stats(v):
            v.tensor_scalar(out=mu, in0=su, scalar1=1.0 / D, scalar2=None,
                            op0=ALU.mult)
            v.tensor_scalar(out=var, in0=sq, scalar1=1.0 / D, scalar2=None,
                            op0=ALU.mult)
            v.drain()
            v.scalar_tensor_tensor(out=t0, in0=mu, scalar=-1.0, in1=mu,
                                   op0=ALU.mult, op1=ALU.mult)
            v.drain()
            v.tensor_tensor(out=var, in0=var, in1=t0, op=ALU.add)
            v.drain()
            v.tensor_scalar(out=var, in0=var, scalar1=float(EPS),
                            scalar2=None, op0=ALU.add)
            return v.drain()
        k.op("vector", stats, waits=[(SE, sqv)], incs=[(VE, 1)])
        k.op("scalar", lambda s: s.activation(out=t0, in_=var, func=AF.Ln),
             waits=[(VE, k.val(VE))], incs=[(SE, 1)])
        k.op("scalar", lambda s: (s.activation(out=t1, in_=t0, func=AF.Exp,
                                               scale=-0.5), s.drain())[-1],
             waits=[(SE, k.val(SE))], incs=[(SE, 1)])

        def fin(v):
            v.tensor_tensor(out=t0, in0=t1, in1=t1, op=ALU.mult)
            v.drain()
            v.tensor_tensor(out=t0, in0=t0, in1=var, op=ALU.mult)
            v.drain()
            v.tensor_scalar(out=t0, in0=t0, scalar1=-0.5, scalar2=1.5,
                            op0=ALU.mult, op1=ALU.add)
            v.drain()
            v.tensor_tensor(out=rstd, in0=t1, in1=t0, op=ALU.mult)
            return v.drain()
        k.op("vector", fin, waits=[(SE, k.val(SE))], incs=[(VE, 1)])
        return mu, rstd, k.val(VE), sqv

    def setup():
        onesf = scr_f[:, 1024:1152]
        k.op("gpsimd", lambda g: g.memset(onesf, 1.0))
        k.op("gpsimd", lambda g: g.affine_select(
            out=identf[:, :], in_=onesf, pattern=[[1, 128]],
            compare_op=ALU.is_equal, fill=0.0, channel_multiplier=-1))
        k.op("gpsimd", lambda g: g.tensor_copy(identb[:, :], identf[:, :]))
        k.op("gpsimd", lambda g: g.affine_select(
            out=scr_f[:, 0:128], in_=onesf, pattern=[[1, 128]], base=-64,
            compare_op=ALU.is_equal, fill=0.0, channel_multiplier=-1))
        k.op("gpsimd", lambda g: g.affine_select(
            out=scr_f[:, 128:256], in_=onesf, pattern=[[1, 128]], base=64,
            compare_op=ALU.is_equal, fill=0.0, channel_multiplier=-1),
            incs=[(GE, 1)])
        k.op("vector", lambda v: v.tensor_sub(
            rotT[:, :], scr_f[:, 0:128], scr_f[:, 128:256]),
            waits=[(GE, k.val(GE))], incs=[(VE, 1)])
        k.op("gpsimd", lambda g: g.memset(scr_f[:, 256:384], NEG),
             waits=[(VE, k.val(VE))], incs=[(GE, 1)])
        k.op("gpsimd", lambda g: g.affine_select(
            out=maskA[:, :], in_=scr_f[:, 256:384], pattern=[[1, 128]],
            compare_op=ALU.is_gt, fill=0.0, channel_multiplier=-1))
        k.op("gpsimd", lambda g: g.affine_select(
            out=maskB[:, :], in_=scr_f[:, 256:384], pattern=[[-1, 128]],
            compare_op=ALU.is_gt, fill=0.0, channel_multiplier=1))
        k.op("gpsimd", lambda g: g.memset(ones1[:, :], 1.0))
        for dst, nm in ((g_t, "ln_g"), (b_t, "ln_b"),
                        (bias_t["bq"], "bq"), (bias_t["bk"], "bk"),
                        (bias_t["bv"], "bv"), (ds_t, "down_s"),
                        (gs_t, "gate_s"), (us_t, "up_s")):
            k.op("sync", lambda s, dst=dst, nm=nm: s.dma_start(
                out=dst[:, :],
                in_=VHBM[nm].rearrange("(c p) -> p c", p=128),
                allow_slow_non_contiguous=True),
                incs=[(DSY, 16)])
        k.op("gpsimd", lambda g: g.dma_start(
            out=bo_row[:, :], in_=VHBM["bo"][None, :]), incs=[(DGP, 16)])
        k.barrier()

    def rope_tables():
        th = st_a[:, 0:1]
        thm = st_a[:, 1:2]
        k.op("gpsimd", lambda g: g.iota(
            th, pattern=[[0, 1]], base=0, channel_multiplier=1,
            allow_small_or_imprecise_dtypes=True), incs=[(GE, 1)])

        def thmod(v):
            v.tensor_scalar(out=thm, in0=th, scalar1=64.0, scalar2=None,
                            op0=ALU.is_ge)
            v.drain()
            v.scalar_tensor_tensor(out=th, in0=thm, scalar=-64.0,
                                   in1=th, op0=ALU.mult, op1=ALU.add)
            return v.drain()
        k.op("vector", thmod, waits=[(GE, k.val(GE))], incs=[(VE, 1)])
        k.op("scalar", lambda s: s.activation(
            out=th, in_=th, func=AF.Exp,
            scale=-float(np.log(10000.0) / 64.0)),
            waits=[(VE, k.val(VE))], incs=[(SE, 1)])
        pos = scr_f[:, 0:1024]
        tmp = scr_f[:, 1024:2048]
        ang = scr_b[:, 0:2048].bitcast(F32)
        for half in range(4):
            k.op("gpsimd", lambda g, half=half: g.iota(
                pos, pattern=[[1, 1024]], base=half * 1024, channel_multiplier=0,
                allow_small_or_imprecise_dtypes=True),
                waits=[(SE, k.val(SE)), (VE, k.val(VE))], incs=[(GE, 1)])
            for tab, shift in ((sin_t, 0.0), (cos_t, float(np.pi / 2))):
                def mk(v, shift=shift):
                    v.tensor_scalar(out=ang, in0=pos, scalar1=th, scalar2=None,
                                    op0=ALU.mult)
                    if shift:
                        v.tensor_scalar(out=ang, in0=ang, scalar1=shift,
                                        scalar2=None, op0=ALU.add)
                    v.tensor_scalar(out=tmp, in0=ang,
                                    scalar1=float(1.0 / (2 * np.pi)),
                                    scalar2=None, op0=ALU.mult)
                    v.tensor_scalar(out=tmp, in0=tmp, scalar1=MAGIC,
                                    scalar2=MAGIC, op0=ALU.add, op1=ALU.subtract)
                    v.scalar_tensor_tensor(
                        out=ang, in0=tmp, scalar=float(-2 * np.pi), in1=ang,
                        op0=ALU.mult, op1=ALU.add)
                    return v.tensor_scalar(
                        out=ang, in0=ang, scalar1=float(np.pi - 1e-6),
                        scalar2=float(-np.pi + 1e-6), op0=ALU.min, op1=ALU.max)
                k.op("vector", mk, waits=[(GE, k.val(GE)), (SE, k.val(SE))],
                     incs=[(VE, 1)])
                k.op("scalar", lambda s, tab=tab, half=half: s.activation(
                    out=tab[:, half * 1024:(half + 1) * 1024], in_=ang,
                    func=AF.Sin), waits=[(VE, k.val(VE))], incs=[(SE, 1)])
        k.barrier()

    def phase_ln(src_hbm, pre, gb):
        fin_at, tr_at, sq_at = {}, {}, {}
        for t in range(NT):
            xa, za = xt[t % 2], ztm[t % 2]
            w = list(pre) if t == 0 else []
            if t >= 2:
                w.append((VE, fin_at[t - 2]))
                w.append((SE, sq_at[t - 2]))
                if "st" in dbg:
                    w.append((DSY, k.val(DSY)))
            k.op("sync", lambda s, t=t, xa=xa: s.dma_start(
                out=xa[:, :], in_=src_hbm[t * 128:(t + 1) * 128, :]),
                waits=w, incs=[(DSY, 16)])
            ld = k.val(DSY)
            war = [(DSY, ld)]
            if t >= 2:
                war.append((VE, fin_at[t - 2]))
            mu, rstd, stats_ve, sq_at[t] = ln_stats(xa, t % 2, war)
            if "st" in dbg and src_hbm is x1_hbm:
                k.op("sync", lambda s, t=t, par=t % 2: s.dma_start(
                    out=dbg["st"][t], in_=st_a[:, 8 * par:8 * par + 8]),
                    waits=[(VE, stats_ve)], incs=[(DSY, 16)])
            zw = [(VE, stats_ve)]
            if t >= 2:
                zw.append((TE, tr_at[t - 2]))
            k.op("vector", lambda v, xa=xa, za=za, mu=mu, rstd=rstd:
                 v.tensor_scalar(out=za[:, :], in0=xa[:, :], scalar1=mu,
                                 scalar2=rstd, op0=ALU.subtract, op1=ALU.mult),
                 waits=zw, incs=[(VE, 1)])
            fin_at[t] = k.val(VE)
            transpose_seq(
                NDM, lambda c, za=za: za[:, c * 128:(c + 1) * 128],
                lambda c, t=t: zfm[:, c * TC + t * 128:c * TC + (t + 1) * 128],
                3, [(VE, fin_at[t])], evac="scalar", fuse_gb=gb)
            tr_at[t] = k.val(TE)
        k.barrier()

    wstate = {"rr": 0, "nb_free": {}}

    def stream_wslab(dst, w_hbm, rows, scale_col, first, cache_ap, dst_free,
                     trq, trq_wait):
        if not first:
            k.op("sync", lambda s: s.dma_start(out=dst[:, :], in_=cache_ap),
                 waits=dst_free, incs=[(DSY, 16)])
            return (DSY, k.val(DSY))
        nb = wnatb[wstate["rr"] % 4]
        wstate["rr"] += 1
        w = [wstate["nb_free"][id(nb)]] if id(nb) in wstate["nb_free"] else []
        k.op("gpsimd", lambda g: g.dma_start(out=nb[:, :], in_=w_hbm[rows, :]),
             waits=w, incs=[(DGP, 16)])
        ldv = k.val(DGP)
        if scale_col is not None:
            k.op("vector", lambda v: v.tensor_scalar(
                out=nb[:, :], in0=nb[:, :], scalar1=scale_col, scalar2=None,
                op0=ALU.mult), waits=[(DGP, ldv)], incs=[(VE, 1)])
            ready = [(VE, k.val(VE))]
        else:
            ready = [(DGP, ldv)]
        done = transpose_seq(
            NDM, lambda i, nb=nb: nb[:, i * 128:(i + 1) * 128],
            lambda i, dst=dst: dst[:, i * 128:(i + 1) * 128],
            trq, ready + list(dst_free) + list(trq_wait), evac="scalar")
        wstate["nb_free"][id(nb)] = (TE, k.val(TE))
        if cache_ap is not None:
            k.op("sync", lambda s: s.dma_start(out=cache_ap, in_=dst[:, :]),
                 waits=[done], incs=[(DSY, 16)])
        return done

    def stream_cslab(dst, w_hbm, ic, scale_t, first, cache_ap, dst_free, trq,
                     trq_wait):
        if not first:
            k.op("sync", lambda s: s.dma_start(out=dst[:, :], in_=cache_ap),
                 waits=dst_free, incs=[(DSY, 16)])
            return (DSY, k.val(DSY))
        nb = wnatb[wstate["rr"] % 4]
        wstate["rr"] += 1
        w = [wstate["nb_free"][id(nb)]] if id(nb) in wstate["nb_free"] else []
        k.op("gpsimd", lambda g, ic=ic, nb=nb: g.dma_start(
            out=nb.rearrange("p (c i) -> p c i", c=NDM),
            in_=w_hbm.rearrange(
                "(c p) (i r) -> p c i r", p=128, r=128)[:, :, ic, :]),
            waits=w, incs=[(DGP, 16)])
        ldv = k.val(DGP)
        if scale_t is not None:
            def sc(v, nb=nb):
                inst = None
                for c in range(NDM):
                    inst = v.tensor_scalar(
                        out=nb[:, c * 128:(c + 1) * 128],
                        in0=nb[:, c * 128:(c + 1) * 128],
                        scalar1=scale_t[:, c:c + 1], scalar2=None, op0=ALU.mult)
                return inst
            k.op("vector", sc, waits=[(DGP, ldv)], incs=[(VE, 1)])
            ready = [(VE, k.val(VE))]
        else:
            ready = [(DGP, ldv)]
        done = transpose_seq(
            NDM, lambda i, nb=nb: nb[:, i * 128:(i + 1) * 128],
            lambda i, dst=dst: dst[:, i * 128:(i + 1) * 128],
            trq, ready + list(dst_free) + list(trq_wait), evac="scalar")
        wstate["nb_free"][id(nb)] = (TE, k.val(TE))
        if cache_ap is not None:
            k.op("sync", lambda s, dst=dst, cache_ap=cache_ap: s.dma_start(
                out=cache_ap, in_=dst[:, :]), waits=[done], incs=[(DSY, 16)])
        return done

    def phase_ffn(first, res_src_hbm, res_dst_hbm):
        k.wait("sync", [(DSY, k.val(DSY)), (DGP, k.val(DGP))])
        ic0 = 0
        for third, nics in enumerate(QUARTERS):
            ics = list(range(ic0, ic0 + nics))
            ic0 += nics
            h_done = {}
            slab_free = {i: () for i in range(4)}
            for j, ic in enumerate(ics):
                gT, uT = wT[2 * (j % 2)], wT[2 * (j % 2) + 1]
                trw = [(VE, h_done.get((j - 1, 0), 0))]
                wg = stream_wslab(gT, WHBM["gate_w"],
                                  slice(ic * 128, (ic + 1) * 128),
                                  gs_t[:, ic:ic + 1], first, cache["gate_w"][ic],
                                  slab_free[2 * (j % 2)], 1, trw)
                wu = stream_wslab(uT, WHBM["up_w"],
                                  slice(ic * 128, (ic + 1) * 128),
                                  us_t[:, ic:ic + 1], first, cache["up_w"][ic],
                                  slab_free[2 * (j % 2) + 1], 1, trw)
                for th_ in range(2):
                    pg, pu = Pp[th_ * 2], Pp[th_ * 2 + 1]
                    w = [wg, wu]
                    if (j - 1, th_) in h_done:
                        w.append((VE, h_done[(j - 1, th_)]))

                    def mk(t, gT=gT, uT=uT, th_=th_, pg=pg, pu=pu):
                        inst = None
                        for ps, wt in ((pg, gT), (pu, uT)):
                            for c in range(NDM):
                                for ns in range(2):
                                    off = c * TC + th_ * 1024 + ns * 512
                                    inst = t.matmul(
                                        ps[:, ns * 512:(ns + 1) * 512],
                                        wt[:, c * 128:(c + 1) * 128],
                                        zfm[:, off:off + 512],
                                        start=(c == 0), stop=(c == NDM - 1))
                        return inst
                    k.op("tensor", mk, waits=w, incs=[(TE, 1)])
                    mmv = k.val(TE)
                    k.op("scalar", lambda s, th_=th_, pg=pg: s.activation(
                        out=silu_b[:, th_ * 1024:(th_ + 1) * 1024], in_=pg[:, :],
                        func=AF.Silu), waits=[(TE, mmv)], incs=[(SE, 1)])
                    k.op("vector", lambda v, j=j, th_=th_, pu=pu:
                         v.tensor_tensor(
                             out=h_ar[:, j * TC + th_ * 1024:
                                      j * TC + (th_ + 1) * 1024],
                             in0=pu[:, :],
                             in1=silu_b[:, th_ * 1024:(th_ + 1) * 1024],
                             op=ALU.mult),
                         waits=[(SE, k.val(SE)), (TE, mmv)], incs=[(VE, 1)])
                    h_done[(j, th_)] = k.val(VE)
                slab_free[2 * (j % 2)] = [(TE, k.val(TE))]
                slab_free[2 * (j % 2) + 1] = [(TE, k.val(TE))]
            lastg = [(VE, h_done[(len(ics) - 1, 0)]),
                     (VE, h_done[(len(ics) - 1, 1)])]
            if third == 0 and "h0" in dbg:
                k.op("sync", lambda s: s.dma_start(
                    out=dbg["h0"].transpose([1, 0, 2]),
                    in_=h_ar.rearrange("p (j t) -> p j t", j=11)),
                    waits=[(VE, k.val(VE))], incs=[(DSY, 16)])
            dsl_ready = []
            for j, ic in enumerate(ics):
                trw = lastg if j == 0 else []
                dslf = [(TE, k.val(TE)), (SE, k.val(SE))] if not first else []
                dsl_ready.append(stream_cslab(
                    dsl[j], WHBM["down_w"], ic, ds_t, first,
                    cache["down_w"][ic], dslf, 3, trw))
            stage_done = (SE, k.val(SE)) if first else (DSY, k.val(DSY))
            ev_at = {0: 0, 1: 0}
            for tg in range(NT):
                par = tg % 2
                pa, pb = Pp[par * 2], Pp[par * 2 + 1]
                w = list(dsl_ready) + [stage_done] if tg == 0 else []
                if tg == 1:
                    w = [stage_done]
                if tg >= 2:
                    w.append((VE, ev_at[par]))

                def mk(t, tg=tg, pa=pa, pb=pb, nics=nics):
                    inst = None
                    for j in range(nics):
                        hT = h_ar[:, j * TC + tg * 128:j * TC + (tg + 1) * 128]
                        for ns in range(4):
                            ps = pa if ns < 2 else pb
                            inst = t.matmul(
                                ps[:, (ns % 2) * 512:(ns % 2) * 512 + 512],
                                hT, dsl[j][:, ns * 512:(ns + 1) * 512],
                                start=(j == 0), stop=(j == nics - 1))
                    return inst
                k.op("tensor", mk, waits=w, incs=[(TE, 1)])
                mmv = k.val(TE)
                ra = rt[par]
                rows = slice(tg * 128, (tg + 1) * 128)
                if third == 0:
                    xa = xt[par]
                    k.op("sync", lambda s, rows=rows, xa=xa: s.dma_start(
                        out=xa[:, :], in_=res_src_hbm[rows, :]),
                        waits=[(VE, ev_at[par])] if tg >= 2 else [],
                        incs=[(DSY, 16)])
                    ldx = k.val(DSY)

                    def ev(v, xa=xa, ra=ra, pa=pa, pb=pb):
                        v.scalar_tensor_tensor(
                            out=ra[:, 0:1024], in0=pa[:, :], scalar=0.5,
                            in1=xa[:, 0:1024], op0=ALU.mult, op1=ALU.add)
                        return v.scalar_tensor_tensor(
                            out=ra[:, 1024:2048], in0=pb[:, :], scalar=0.5,
                            in1=xa[:, 1024:2048], op0=ALU.mult, op1=ALU.add)
                    k.op("vector", ev, waits=[(TE, mmv), (DSY, ldx)],
                         incs=[(VE, 1)])
                else:
                    def ev(v, ra=ra, pa=pa, pb=pb):
                        v.tensor_scalar(out=ra[:, 0:1024], in0=pa[:, :],
                                        scalar1=0.5, scalar2=None, op0=ALU.mult)
                        return v.tensor_scalar(out=ra[:, 1024:2048],
                                               in0=pb[:, :], scalar1=0.5,
                                               scalar2=None, op0=ALU.mult)
                    wv = [(TE, mmv)]
                    if tg >= 2:
                        wv.append((VE, ev_at[par]))
                    k.op("vector", ev, waits=wv, incs=[(VE, 1)])
                ev_at[par] = k.val(VE)
                if third == 0:
                    k.op("sync", lambda s, ra=ra, rows=rows: s.dma_start(
                        out=res_dst_hbm[rows, :], in_=ra[:, :]),
                        waits=[(VE, ev_at[par])], incs=[(DSY, 16)])
                else:
                    k.op("gpsimd", lambda g, ra=ra, rows=rows: g.dma_start(
                        out=res_dst_hbm[rows, :], in_=ra[:, :],
                        accum_op=ALU.add),
                        waits=[(VE, ev_at[par])], incs=[(DGP, 16)])
            k.barrier()
        k.barrier()

    qkv_store_done = {}

    def phase_qkv():
        k.wait("sync", [(DSY, k.val(DSY)), (DGP, k.val(DGP))])
        pfree = {0: 0, 1: 0, 2: 0}
        sfree = {0: 0, 1: 0}
        slab_free = {i: () for i in range(4)}
        it = 0
        for ip, wnm in enumerate(("wq", "wk", "wv")):
            bcol = bias_t[("bq", "bk", "bv")[ip]]
            for oc in range(NH):
                dst = qwT[it % 4]
                wready = stream_wslab(dst, WHBM[wnm],
                                      slice(oc * 128, (oc + 1) * 128), None,
                                      True, None, slab_free[it % 4], 3, ())
                for th_ in range(2):
                    pq = Pp[(2 * it + th_) % 3]

                    def mk(t, dst=dst, th_=th_, pq=pq):
                        inst = None
                        for c in range(NDM):
                            for ns in range(2):
                                off = c * TC + th_ * 1024 + ns * 512
                                inst = t.matmul(
                                    pq[:, ns * 512:(ns + 1) * 512],
                                    dst[:, c * 128:(c + 1) * 128],
                                    zfm[:, off:off + 512],
                                    start=(c == 0), stop=(c == NDM - 1))
                        return inst
                    w = [wready]
                    if pfree[(2 * it + th_) % 3]:
                        w.append((VE, pfree[(2 * it + th_) % 3]))
                    k.op("tensor", mk, waits=w, incs=[(TE, 1)])
                    mmv = k.val(TE)
                    sb = qst[th_]
                    wv = [(TE, mmv)]
                    if sfree[th_]:
                        wv.append((DSC, sfree[th_]))
                    k.op("vector", lambda v, sb=sb, pq=pq, bcol=bcol, oc=oc:
                         v.tensor_scalar(out=sb[:, :], in0=pq[:, :],
                                         scalar1=bcol[:, oc:oc + 1],
                                         scalar2=None, op0=ALU.add),
                         waits=wv, incs=[(VE, 1)])
                    pfree[(2 * it + th_) % 3] = k.val(VE)
                    k.op("scalar", lambda s, sb=sb, ip=ip, oc=oc, th_=th_:
                         s.dma_start(
                             out=qkv_send[ip, oc // 2, oc % 2, :,
                                          th_ * 1024:(th_ + 1) * 1024],
                             in_=sb[:, :]),
                         waits=[(VE, k.val(VE))], incs=[(DSC, 16)])
                    sfree[th_] = k.val(DSC)
                slab_free[it % 4] = [(TE, k.val(TE))]
                it += 1
            qkv_store_done[ip] = k.val(DSC)
        k.barrier()

    def phase_exchange(send_ap, recv_ap, wait=None):
        if wait is None:
            wait = [(DSC, k.val(DSC)), (DSY, k.val(DSY))]
        k.wait("gpsimd", wait)

        def f(g):
            with allow_ici_all_to_all():
                return g.collective_compute(
                    "AllToAll", ALU.bypass,
                    replica_groups=[list(range(NCORES))],
                    ins=[send_ap.opt()], outs=[recv_ap.opt()])
        k.op("gpsimd", f, incs=[(CC, 1)])

    def phase_attention():
        k.wait("sync", [(CC, k.val(CC))])
        st8 = {"qkv_free": {0: 0, 1: 0},
               "post_a": 0,
               "post_b": 0,
               "po_evac": {0: 0, 1: 0},
               "pt_guard": 0}

        def emit_pro_a(u):
            b_, hh = u // 2, u % 2
            par = u % 2
            aq, ak, avt = at_q[par], at_k[par], at_vt[par]
            thunks = []
            tfree = st8["qkv_free"][par]
            vfree = st8.get("vt_te", 0)
            for dst, ip in ((aq, 0), (ak, 1), (at_v, 2)):
                for hb in range(2):
                    def ld(dst=dst, ip=ip, hb=hb, b_=b_, hh=hh, tfree=tfree,
                           vfree=vfree):
                        w = [(TE, max(tfree, vfree) if ip == 2 else tfree)]
                        k.op("sync", lambda s: s.dma_start(
                            out=dst[:, hb * TC:(hb + 1) * TC],
                            in_=qkv_recv[ip, 2 * b_ + hb, hh, :, :]),
                            waits=w, incs=[(DSY, 16)])
                        st8["ldv"] = k.val(DSY)
                    thunks.append(ld)
            def mk_rope(ti, sl):
                tens = aq if ti == 0 else ak
                def f(tens=tens, sl=sl, ti=ti):
                    cs = slice(sl * 512, (sl + 1) * 512)
                    pq = Pp[0][:, (sl % 2) * 512:(sl % 2) * 512 + 512]
                    w = [(DSY, st8["ldv"]), (SE, st8["post_a"]),
                         (SE, st8["post_b"])] if (ti == 0 and sl == 0) else []
                    if st8.get("rve2") and (ti, sl) not in ((0, 0), (0, 1)):
                        pr = (ti, sl - 2) if sl >= 2 else (ti - 1, 6 + sl)
                        if pr in st8["rve2"]:
                            w.append((VE, st8["rve2"][pr]))
                    k.op("tensor", lambda t, tens=tens, cs=cs, pq=pq: t.matmul(
                        pq, rotT[:, :], tens[:, cs], start=True, stop=True),
                        waits=w, incs=[(TE, 1)])
                    trv = k.val(TE)
                    def rv(v, tens=tens, cs=cs, pq=pq):
                        v.tensor_tensor(out=at_rt[:, :], in0=pq,
                                        in1=sin_t[:, cs], op=ALU.mult)
                        v.tensor_tensor(out=tens[:, cs], in0=tens[:, cs],
                                        in1=cos_t[:, cs], op=ALU.mult)
                        return v.tensor_tensor(out=tens[:, cs], in0=tens[:, cs],
                                               in1=at_rt[:, :], op=ALU.add)
                    k.op("vector", rv, waits=[(DSY, st8["ldv"]), (TE, trv)],
                         incs=[(VE, 1)])
                    st8["rve2"][(ti, sl)] = k.val(VE)
                return f
            st8["rve2"] = {}
            for ti in range(2):
                for sl in range(8):
                    thunks.append(mk_rope(ti, sl))
            vstate = {}
            def mk_vt(i):
                def f(i=i):
                    q = Pp[1].bitcast(BF16)[:, (i % 2) * 1024:(i % 2) * 1024 + 128]
                    w = [(DSY, st8["ldv"]), (SE, st8["post_a"]),
                         (SE, st8["post_b"])] if i == 0 else []
                    if i >= 2:
                        w.append((SE, vstate[i - 2]))
                    k.op("tensor", lambda t, i=i, q=q: t.transpose(
                        q, at_v[:, i * 128:(i + 1) * 128], identb[:, :]),
                        waits=w, incs=[(TE, 1)])
                    k.op("scalar", lambda s, i=i, q=q: s.copy(
                        out=avt[:, i * 128:(i + 1) * 128], in_=q),
                        waits=[(TE, k.val(TE))], incs=[(SE, 1)])
                    vstate[i] = k.val(SE)
                    st8["vt_se"] = k.val(SE)
                    st8["vt_te"] = k.val(TE)
                return f
            for i in range(32):
                thunks.append(mk_vt(i))
            astate = {"exp": {0: None, 1: None}, "vtiny": 0, "abuf": 0}
            def mk_a(j, ci, k0, w_, nch):
                def f(j=j, ci=ci, k0=k0, w_=w_, nch=nch):
                    par2 = astate["abuf"] % 2
                    astate["abuf"] += 1
                    pa = Pp[par2]
                    def mka(t, j=j, k0=k0, w_=w_, pa=pa):
                        inst = None
                        for off in range(0, w_, 512):
                            n = min(512, w_ - off)
                            inst = t.matmul(
                                pa[:, off:off + n],
                                aq[:, j * 128:(j + 1) * 128],
                                ak[:, k0 + off:k0 + off + n],
                                start=True, stop=True)
                        return inst
                    if astate["exp"][par2] is None:
                        w2 = [(VE, st8["rve2"][(1, 7)]), (SE, st8["vt_se"])]
                    else:
                        w2 = [(SE, astate["exp"][par2])]
                    k.op("tensor", mka, waits=w2, incs=[(TE, 1)])
                    mmv = k.val(TE)
                    wmask = [(TE, mmv)]
                    if k0 + w_ == (j + 1) * 128:
                        dcol = w_ - 128
                        k.op("vector", lambda v, pa=pa, dcol=dcol:
                             v.tensor_tensor(out=pa[:, dcol:dcol + 128],
                                             in0=pa[:, dcol:dcol + 128],
                                             in1=maskA[:, :], op=ALU.add),
                             waits=[(TE, mmv), (VE, astate["vtiny"])],
                             incs=[(VE, 1)])
                        wmask = [(VE, k.val(VE))]
                    cm = at_t[:, 4:5]
                    nbc = at_t[:, 6:7]
                    csum = at_t[:, 7:8]
                    def redmax(v, pa=pa, w_=w_, cm=cm, j=j, first=(ci == 0)):
                        if not first:
                            v.tensor_copy(at_t[:, 0:1], at_m[:, j:j + 1])
                        v.reduce_max(cm, pa[:, 0:w_], axis=AXX)
                        v.drain()
                        if first:
                            v.tensor_copy(at_m[:, j:j + 1], cm)
                        else:
                            v.tensor_max(at_m[:, j:j + 1], at_m[:, j:j + 1], cm)
                        v.drain()
                        v.tensor_scalar(out=nbc, in0=at_m[:, j:j + 1],
                                        scalar1=-ISQ, scalar2=None,
                                        op0=ALU.mult)
                        return v.drain()
                    k.op("vector", redmax,
                         waits=wmask + [(VE, astate["vtiny"])], incs=[(VE, 1)])
                    astate["vtiny"] = k.val(VE)
                    k.op("scalar", lambda s, pa=pa, w_=w_, nbc=nbc, csum=csum:
                         s.activation(out=aexp_b[:, 0:w_], in_=pa[:, 0:w_],
                                      func=AF.Exp, bias=nbc, scale=ISQ,
                                      accum_out=csum),
                         waits=[(VE, astate["vtiny"])], incs=[(SE, 1)])
                    astate["exp"][par2] = k.val(SE)
                    if ci == 0:
                        k.op("vector", lambda v, j=j, csum=csum: (
                            v.tensor_copy(at_s[:, j:j + 1], csum),
                            v.drain())[-1],
                            waits=[(SE, astate["exp"][par2])], incs=[(VE, 1)])
                    else:
                        def upd(v, j=j):
                            v.tensor_sub(at_t[:, 1:2], at_t[:, 0:1],
                                         at_m[:, j:j + 1])
                            v.drain()
                            v.tensor_scalar(out=at_t[:, 1:2], in0=at_t[:, 1:2],
                                            scalar1=ISQ, scalar2=None,
                                            op0=ALU.mult)
                            return v.drain()
                        k.op("vector", upd, waits=[(SE, astate["exp"][par2])],
                             incs=[(VE, 1)])
                        k.op("scalar", lambda s: (s.activation(
                            out=at_t[:, 1:2], in_=at_t[:, 1:2], func=AF.Exp),
                            s.drain())[-1],
                            waits=[(VE, k.val(VE))], incs=[(SE, 1)])
                        def upd2(v, j=j, csum=csum):
                            v.tensor_tensor(out=at_s[:, j:j + 1],
                                            in0=at_s[:, j:j + 1],
                                            in1=at_t[:, 1:2], op=ALU.mult)
                            v.drain()
                            v.tensor_tensor(out=at_s[:, j:j + 1],
                                            in0=at_s[:, j:j + 1], in1=csum,
                                            op=ALU.add)
                            return v.drain()
                        k.op("vector", upd2, waits=[(SE, k.val(SE))],
                             incs=[(VE, 1)])
                    astate["vtiny"] = k.val(VE)
                    st8["post_a"] = max(x for x in astate["exp"].values() if x)
                return f
            for j in range(32):
                end = (j + 1) * 128
                chunks = [(s0, min(1024, end - s0))
                          for s0 in range(0, end, 1024)]
                for ci, (k0, w_) in enumerate(chunks):
                    thunks.append(mk_a(j, ci, k0, w_, len(chunks)))
            return thunks

        def emit_nm(u):
            def nmrow(v):
                v.tensor_copy(at_mt[:, 0:32], at_s[:, 0:32])
                return v.drain()
            k.op("vector", nmrow, waits=[(VE, k.val(VE)), (SE, k.val(SE))],
                 incs=[(VE, 1)])
            k.op("scalar", lambda s: (s.activation(
                out=at_mt[:, 0:32], in_=at_mt[:, 0:32], func=AF.Ln),
                s.drain())[-1], waits=[(VE, k.val(VE))], incs=[(SE, 1)])
            def nmrow2(v):
                v.scalar_tensor_tensor(out=at_mt[:, 0:32], in0=at_mt[:, 0:32],
                                       scalar=SQ128, in1=at_m[:, 0:32],
                                       op0=ALU.mult, op1=ALU.add)
                v.drain()
                v.tensor_scalar(out=at_mt[:, 0:32], in0=at_mt[:, 0:32],
                                scalar1=-1.0, scalar2=None, op0=ALU.mult)
                v.drain()
                v.tensor_copy(at_hi[:, 0:32], at_mt[:, 0:32])
                v.drain()
                v.tensor_tensor(out=at_mt[:, 0:32], in0=at_mt[:, 0:32],
                                in1=at_hi[:, 0:32], op=ALU.subtract)
                v.drain()
                v.tensor_copy(at_lo[:, 0:32], at_mt[:, 0:32])
                return v.drain()
            k.op("vector", nmrow2, waits=[(SE, k.val(SE))], incs=[(VE, 1)])
            nm_ve = k.val(VE)
            for hi_lo, (srct, cols) in enumerate(((at_hi, slice(0, 128)),
                                                  (at_lo, slice(128, 256)))):
                q = Pp[2].bitcast(BF16)[:, hi_lo * 1024:hi_lo * 1024 + 128]
                wnm = [(VE, nm_ve), (SE, st8["post_a"]), (SE, st8["post_b"])] \
                    if hi_lo == 0 else []
                k.op("tensor", lambda t, srct=srct, q=q: t.transpose(
                    q[0:32, :], srct[:, 0:32], identb[:, :]),
                    waits=wnm, incs=[(TE, 1)])
                k.op("scalar", lambda s, cols=cols, q=q: s.copy(
                    out=at_nmt[:, cols], in_=q[0:32, :]),
                    waits=[(TE, k.val(TE))], incs=[(SE, 1)])
            nm_se = k.val(SE)
            k.op("scalar", lambda s: s.dma_start(
                out=nm_bounce.transpose([1, 0, 2]),
                in_=at_nmt.rearrange("q (h i) -> q h i", h=2)),
                waits=[(SE, nm_se)], incs=[(DSC, 16)])
            k.op("sync", lambda s: s.dma_start(
                out=at_nmh[:, :],
                in_=nm_bounce[0].rearrange("q i -> (q i)")[None, :]),
                waits=[(DSC, k.val(DSC)), (TE, k.val(TE))], incs=[(DSY, 16)])
            k.op("sync", lambda s: s.dma_start(
                out=at_nml[:, :],
                in_=nm_bounce[1].rearrange("q i -> (q i)")[None, :]),
                incs=[(DSY, 16)])
            return (DSY, k.val(DSY))

        def emit_bpv(u, nm_ready):
            b_, hh = u // 2, u % 2
            par = u % 2
            aq, ak, avt = at_q[par], at_k[par], at_vt[par]
            bstate = {"pt": {0: None, 1: None}, "exp": {0: None, 1: None},
                      "bbuf": 0}
            thunks = []
            def mk_b(g8, kc, nkc):
                def f(g8=g8, kc=kc, nkc=nkc):
                    po = Pp[3][:, (g8 % 2) * 512:(g8 % 2) * 512 + 512]
                    par2 = bstate["bbuf"] % 2
                    bstate["bbuf"] += 1
                    pb = Pp[2][:, par2 * 512:par2 * 512 + 512]
                    def mkb(t, kc=kc, g8=g8, pb=pb):
                        t.matmul(pb, ak[:, kc * 128:(kc + 1) * 128],
                                 aq[:, g8 * 512:(g8 + 1) * 512],
                                 start=True, stop=False)
                        t.matmul(pb, ones1[0:1, 0:128],
                                 at_nmh[0:1, g8 * 512:(g8 + 1) * 512],
                                 start=False, stop=False)
                        return t.matmul(pb, ones1[0:1, 0:128],
                                        at_nml[0:1, g8 * 512:(g8 + 1) * 512],
                                        start=False, stop=True)
                    if bstate["exp"][par2] is None:
                        w = [nm_ready, (SE, st8["post_a"])]
                    else:
                        w = [(SE, bstate["exp"][par2])]
                    k.op("tensor", mkb, waits=w, incs=[(TE, 1)])
                    mmv = k.val(TE)
                    dq = kc - 4 * g8
                    wexp = [(TE, mmv)]
                    if 0 <= dq < 4:
                        k.op("vector", lambda v, pb=pb, dq=dq: v.tensor_tensor(
                            out=pb[:, dq * 128:(dq + 1) * 128],
                            in0=pb[:, dq * 128:(dq + 1) * 128],
                            in1=maskB[:, :], op=ALU.add),
                            waits=[(TE, mmv)], incs=[(VE, 1)])
                        wexp = [(VE, k.val(VE))]
                    pt = at_p[par2]
                    if bstate["pt"][par2]:
                        wexp = wexp + [bstate["pt"][par2]]
                    k.op("scalar", lambda s, pb=pb, pt=pt: s.activation(
                        out=pt[:, :], in_=pb, func=AF.Exp, scale=ISQ),
                        waits=wexp, incs=[(SE, 1)])
                    bstate["exp"][par2] = k.val(SE)
                    st8["post_b"] = k.val(SE)
                    nz = max(0, min(kc - 4 * g8, 4))
                    if nz > 0:
                        k.op("vector", lambda v, pt=pt, nz=nz: v.memset(
                            pt[:, 0:nz * 128], 0.0),
                            waits=[(SE, bstate["exp"][par2])], incs=[(VE, 1)])
                        ptw = (VE, k.val(VE))
                    else:
                        ptw = (SE, bstate["exp"][par2])
                    wpv = [ptw]
                    if kc == 0 and st8["po_evac"][g8 % 2]:
                        wpv.append((SE, st8["po_evac"][g8 % 2]))
                    k.op("tensor", lambda t, kc=kc, po=po, pt=pt,
                         first=(kc == 0), last=(kc == nkc - 1): t.matmul(
                             po, avt[:, kc * 128:(kc + 1) * 128], pt[:, :],
                             start=first, stop=last),
                         waits=wpv, incs=[(TE, 1)])
                    bstate["pt"][par2] = (TE, k.val(TE))
                    if kc == nkc - 1:
                        ob = at_o[g8 % 2]
                        k.op("scalar", lambda s, ob=ob, po=po: s.copy(
                            out=ob[:, :], in_=po),
                            waits=[(TE, k.val(TE)), (DSC, k.val(DSC))],
                            incs=[(SE, 1)])
                        st8["po_evac"][g8 % 2] = k.val(SE)
                        k.op("scalar", lambda s, ob=ob, g8=g8, b_=b_, hh=hh:
                             s.dma_start(
                                 out=o_send[2 * b_ + g8 // 4, hh, :,
                                            (g8 % 4) * 512:(g8 % 4) * 512 + 512],
                                 in_=ob[:, :]),
                             waits=[(SE, st8["po_evac"][g8 % 2])],
                             incs=[(DSC, 16)])
                return f
            for g8 in range(8):
                nkc = 4 * g8 + 4
                for kc in range(nkc):
                    thunks.append(mk_b(g8, kc, nkc))
            return thunks

        prev_bpv = []
        for u in range(8):
            pa_thunks = emit_pro_a(u)
            na, nb = len(pa_thunks), len(prev_bpv)
            ia = ib = 0
            while ia < na or ib < nb:
                take_b = ib < nb and (ia >= na or ib * na <= ia * nb)
                if take_b:
                    prev_bpv[ib]()
                    ib += 1
                else:
                    pa_thunks[ia]()
                    ia += 1
            nm_ready = emit_nm(u)
            st8["qkv_free"][(u + 1) % 2] = k.val(TE)
            prev_bpv = emit_bpv(u, nm_ready)
        for f in prev_bpv:
            f()
        k.barrier()

    def phase_oproj():
        k.wait("sync", [(CC, k.val(CC))])
        x2w_done = 0
        oproj_sq = {0: 0, 1: 0}
        for half in range(2):
            hbase = 8 * half
            ldw = [(TE, k.val(TE))] if half == 1 else []
            for i in range(8):
                h = hbase + i
                k.op("sync", lambda s, i=i, h=h: s.dma_start(
                    out=ofm[:, i * TC:(i + 1) * TC],
                    in_=o_recv[h // 2, h % 2, :, :]),
                    waits=ldw if i == 0 else [], incs=[(DSY, 16)])
            ldv = k.val(DSY)
            wo_ready = []
            for i in range(8):
                trw = [(VE, k.val(VE))] if (half == 1 and i == 0) else ()
                wo_ready.append(stream_cslab(
                    wsl[i], WHBM["wo"], hbase + i, None, True, None,
                    [(TE, k.val(TE))] if half == 1 and i < 4 else (), 3, trw))
            stage_done = wo_ready[-1]
            ev_at = {0: 0, 1: 0}
            for tg in range(NT):
                par = tg % 2
                pa, pb = Pp[par * 2], Pp[par * 2 + 1]
                w = [stage_done, (DSY, ldv)] if tg <= 1 else []
                if tg >= 2:
                    w.append((VE, ev_at[par]))

                def mk(t, tg=tg, pa=pa, pb=pb, half=half):
                    inst = None
                    for c in range(8):
                        oT = ofm[:, c * TC + tg * 128:c * TC + (tg + 1) * 128]
                        for ns in range(4):
                            ps = pa if ns < 2 else pb
                            inst = t.matmul(
                                ps[:, (ns % 2) * 512:(ns % 2) * 512 + 512], oT,
                                wsl[c][:, ns * 512:(ns + 1) * 512],
                                start=(c == 0),
                                stop=(c == 7 and half == 1))
                    if half == 0:
                        for ns in range(4):
                            ps = pa if ns < 2 else pb
                            inst = t.matmul(
                                ps[:, (ns % 2) * 512:(ns % 2) * 512 + 512],
                                ones1[0:1, 0:128],
                                bo_row[0:1, ns * 512:(ns + 1) * 512],
                                start=False, stop=True)
                    return inst
                k.op("tensor", mk, waits=w, incs=[(TE, 1)])
                mmv = k.val(TE)
                ra = rt[par]
                rows = slice(tg * 128, (tg + 1) * 128)
                if half == 0:
                    xa = xt[par]
                    k.op("sync", lambda s, rows=rows, xa=xa: s.dma_start(
                        out=xa[:, :], in_=x1_hbm[rows, :]),
                        waits=([(VE, ev_at[par]), (SE, oproj_sq[par])]
                               if tg >= 2 else []),
                        incs=[(DSY, 16)])
                    ldx = k.val(DSY)
                    war = [(DSY, ldx)]
                    if tg >= 2:
                        war.append((VE, ev_at[par]))
                    mu, rstd, stats_ve, sq_o = ln_stats(xa, par, war,
                                                        scr=oscr_f)
                    oproj_sq[par] = sq_o

                    def ev(v, xa=xa, ra=ra, pa=pa, pb=pb, mu=mu, rstd=rstd):
                        v.tensor_scalar(out=ra[:, :], in0=xa[:, :], scalar1=mu,
                                        scalar2=rstd, op0=ALU.subtract,
                                        op1=ALU.mult)
                        v.tensor_tensor(out=ra[:, :], in0=ra[:, :],
                                        in1=xa[:, :], op=ALU.add)
                        v.tensor_tensor(out=ra[:, 0:1024], in0=ra[:, 0:1024],
                                        in1=pa[:, :], op=ALU.add)
                        return v.tensor_tensor(out=ra[:, 1024:2048],
                                               in0=ra[:, 1024:2048],
                                               in1=pb[:, :], op=ALU.add)
                    k.op("vector", ev, waits=[(TE, mmv), (VE, stats_ve)],
                         incs=[(VE, 1)])
                    ev_at[par] = k.val(VE)
                    k.op("sync", lambda s, ra=ra, rows=rows: s.dma_start(
                        out=x2_hbm[rows, :], in_=ra[:, :]),
                        waits=[(VE, ev_at[par])], incs=[(DSY, 16)])
                else:
                    def ev(v, ra=ra, pa=pa, pb=pb):
                        v.tensor_copy(ra[:, 0:1024], pa[:, :])
                        return v.tensor_copy(ra[:, 1024:2048], pb[:, :])
                    wv = [(TE, mmv)]
                    if tg >= 2:
                        wv.append((VE, ev_at[par]))
                    k.op("vector", ev, waits=wv, incs=[(VE, 1)])
                    ev_at[par] = k.val(VE)
                    k.op("gpsimd", lambda g, ra=ra, rows=rows: g.dma_start(
                        out=x2_hbm[rows, :], in_=ra[:, :], accum_op=ALU.add),
                        waits=[(VE, ev_at[par]), (DSY, x2w_done)],
                        incs=[(DGP, 16)])
            if half == 0:
                x2w_done = k.val(DSY)
        k.barrier()

    setup()
    if upto >= 1:
        phase_ln(x_in, [(DSY, k.val(DSY))], None)
    if upto >= 2:
        phase_ffn(True, x_in, x1_hbm)
    if upto >= 3:
        phase_ln(x1_hbm, [(DSY, k.val(DSY)), (DGP, k.val(DGP))], (g_t, b_t))
    if upto >= 4:
        phase_qkv()
    if upto >= 5:
        for ip in range(3):
            phase_exchange(qkv_send[ip], qkv_recv[ip])
        rope_tables()
    if upto >= 6:
        phase_attention()
    if upto >= 7:
        phase_exchange(o_send, o_recv)
    if upto >= 8:
        phase_oproj()
    if upto >= 9:
        phase_ln(x2_hbm, [(DSY, k.val(DSY)), (DGP, k.val(DGP))], None)
    if upto >= 10:
        phase_ffn(False, x2_hbm, out)

    if debug:
        def dump(name, src_ap):
            k.op("sync", lambda s, name=name, src_ap=src_ap: s.dma_start(
                out=dbg[name], in_=src_ap),
                waits=[(CC, k.val(CC)), (DGP, k.val(DGP)),
                       (DSC, k.val(DSC))], incs=[(DSY, 16)])
        k.barrier()
        for nm in dbg:
            if nm == "zfm":
                k.op("sync", lambda s: s.dma_start(
                    out=dbg["zfm"].transpose([1, 0, 2]),
                    in_=zfm.rearrange("p (c t) -> p c t", c=NDM)),
                    incs=[(DSY, 16)])
            elif nm == "x1":
                dump("x1", x1_hbm)
            elif nm == "gc":
                dump("gc", cache["gate_w"])
            elif nm == "dc":
                dump("dc", cache["down_w"])
            elif nm == "x2":
                dump("x2", x2_hbm)
            elif nm == "qs":
                dump("qs", qkv_send)
            elif nm == "qr":
                dump("qr", qkv_recv)
            elif nm == "osend":
                dump("osend", o_send)
            elif nm == "orecv":
                dump("orecv", o_recv)
    k.barrier()
    k.wait("sync", [(DSY, k.val(DSY)), (DGP, k.val(DGP)), (DSC, k.val(DSC)),
                    (CC, k.val(CC))])
    k.barrier()

    with nc.Block() as block:
        for eng in ENGINES:
            def reg(e, eng=eng):
                for f in k.q[eng]:
                    f(e)
            getattr(block, eng)(reg)
    return nc


_CACHED = {}


def kernel(**inputs):
    if "nc" not in _CACHED:
        _CACHED["nc"] = build_program()
    nc = _CACHED["nc"]
    x = np.ascontiguousarray(inputs["L0_attn_h"], dtype=np.float32)
    base = {
        "wq": inputs["L0_attn_wq"], "wk": inputs["L0_attn_wk"],
        "wv": inputs["L0_attn_wv"], "wo": inputs["L0_attn_wo"],
        "bq": inputs["L0_attn_bq"], "bk": inputs["L0_attn_bk"],
        "bv": inputs["L0_attn_bv"], "bo": inputs["L0_attn_bo"],
        "ln_g": inputs["L0_attn_ln_g"], "ln_b": inputs["L0_attn_ln_b"],
        "gate_w": inputs["L0_ffn_gate_w"], "up_w": inputs["L0_ffn_up_w"],
        "down_w": inputs["L0_ffn_down_w"], "gate_s": inputs["L0_ffn_gate_s"],
        "up_s": inputs["L0_ffn_up_s"], "down_s": inputs["L0_ffn_down_s"],
    }
    base = {k2: np.ascontiguousarray(v, dtype=np.float32)
            for k2, v in base.items()}
    in_maps = []
    for c in range(NCORES):
        bb, h = c // 2, c % 2
        m = dict(base)
        m["x"] = np.ascontiguousarray(x[bb, h * TC:(h + 1) * TC, :])
        in_maps.append(m)
    res = run_bass_kernel_spmd(nc, in_maps, core_ids=list(range(NCORES)))
    outp = np.empty((B, S, D), dtype=np.float32)
    for c in range(NCORES):
        bb, h = c // 2, c % 2
        outp[bb, h * TC:(h + 1) * TC, :] = res.results[c]["out"]
    return outp
